# version 07
# baseline (speedup 1.0000x reference)
import numpy as np
from contextlib import ExitStack

import concourse.bass as bass
import concourse.mybir as mybir
from concourse.bass_utils import run_bass_kernel_spmd
from concourse.alu_op_type import AluOpType as OP

F32 = mybir.dt.float32
BF16 = mybir.dt.bfloat16
AF = mybir.ActivationFunctionType
AX = mybir.AxisListType

NCORES = 8
B_PER_CORE = 2
S = 2048
D = 1024
H = 16
HD = 64
DL = 256
E = 16
I = 1408
NI = 11
IS = 2816
T = B_PER_CORE * S
NT128 = T // 128
NCH = D // 128
LAM = 0.01
EPS = 1e-5
INV_SQRT_D = 1.0 / 32.0
NTSB = 4
TSB = T // NTSB


class Prog:
    ENGINES = ("sync", "scalar", "vector", "tensor", "gpsimd")

    def __init__(self, nc, stack):
        self.nc = nc
        self.stack = stack
        self.sched = []
        self.sems = {}
        self.satisfied = {e: {} for e in self.ENGINES}
        self.n_ops = {e: 0 for e in self.ENGINES}

    def sem(self, name):
        if name not in self.sems:
            h = self.stack.enter_context(self.nc.semaphore(name))
            self.sems[name] = [h, 0]
        return self.sems[name]

    def count(self, name):
        return self.sems[name][1] if name in self.sems else 0

    def op(self, eng, fn=None, waits=(), inc=None, dma=False):
        wl = []
        for w in waits:
            name, val = (w, None) if isinstance(w, str) else w
            if name not in self.sems:
                continue
            s = self.sems[name]
            v = s[1] if val is None else val
            assert v <= s[1], f"wait {name}>={v} exceeds scheduled {s[1]}"
            if v <= 0 or self.satisfied[eng].get(name, 0) >= v:
                continue
            self.satisfied[eng][name] = v
            wl.append((s[0], v))
        incs = None
        if inc is not None:
            s = self.sem(inc)
            k = 16 if dma else 1
            s[1] += k
            incs = (s[0], k)
        self.sched.append((eng, wl, fn, incs))
        self.n_ops[eng] += 1

    def barrier(self):
        names = list(self.sems.keys())
        for e in self.ENGINES:
            self.op(e, None, waits=names)

    def emit(self, block):
        decos = {"sync": block.sync, "scalar": block.scalar,
                 "vector": block.vector, "tensor": block.tensor,
                 "gpsimd": block.gpsimd}

        def make_body(name):
            def body(eng):
                for (en, wl, fn, incs) in self.sched:
                    if en != name:
                        continue
                    for (h, v) in wl:
                        eng.wait_ge(h, v)
                    if fn is None:
                        continue
                    r = fn(eng)
                    if incs is not None:
                        r.then_inc(incs[0], incs[1])
            return body
        for name in self.ENGINES:
            decos[name](make_body(name))


def cstep(e, ch, inst):
    inst.then_inc(ch[0], 1)
    ch[1] += 1
    e.wait_ge(ch[0], ch[1])
    return inst


DEV_DUMPS = False
DEV_NEXPERTS = E + 2


def build_program(dev_phase=3):
    nc = bass.Bass("TRN2", target_bir_lowering=False, debug=False)
    dk = "ExternalOutput" if dev_phase < 3 else "Internal"

    d = {}
    d["x"] = nc.dram_tensor("x", [T, D], F32, kind="ExternalInput").ap()
    for nm, shape in [("wq", [D, D]), ("wk", [D, D]), ("wv", [D, D]),
                      ("wo", [D, D]), ("wdown", [D, DL]), ("wupk", [DL, D]),
                      ("wupv", [DL, D]), ("decay", [S, S]), ("gwT", [D, E]),
                      ("gbias", [1, E]), ("tinyc", [1, E]),
                      ("gpT", [E, D, I]), ("upT", [E, D, I]),
                      ("dpT", [E, I, D]), ("sgT", [D, IS]), ("suT", [D, IS]),
                      ("sdT", [IS, D])]:
        d[nm] = nc.dram_tensor(nm, shape, F32, kind="ExternalInput").ap()
    d["xbf"] = nc.dram_tensor("xbf_s", [T, D], BF16, kind=dk).ap()
    d["qf"] = nc.dram_tensor("qf_s", [D, T], BF16, kind=dk).ap()
    d["kf"] = nc.dram_tensor("kf_s", [D, T], BF16, kind=dk).ap()
    d["c0b"] = nc.dram_tensor("c0b_s", [32], F32, kind=dk).ap()
    if dev_phase == 1:
        d["nrm"] = nc.dram_tensor("nrm_s", [8, 2, 16], F32,
                                  kind="ExternalOutput").ap()
        d["dbg"] = nc.dram_tensor("dbg_s", [32, 512], F32,
                                  kind="ExternalOutput").ap()
    yk = dk if dev_phase == 2 else ("Internal" if dev_phase >= 3 else dk)
    d["y1b"] = nc.dram_tensor("y1b_s", [T, D], BF16, kind=yk).ap()
    d["y1f"] = nc.dram_tensor("y1f_s", [T, D], F32, kind=yk).ap()
    d["out"] = nc.dram_tensor("out", [T, D], F32, kind="ExternalOutput").ap()
    if DEV_DUMPS and dev_phase >= 3:
        d["actd"] = nc.dram_tensor("actd_s", [128, NI, TSB], BF16,
                                   kind="ExternalOutput").ap()
        d["sgd"] = nc.dram_tensor("sgd_s", [128, NI, TSB], BF16,
                                  kind="ExternalOutput").ap()
        d["cwd"] = nc.dram_tensor("cwd_s", [NTSB, 128, TSB // 128, E], F32,
                                  kind="ExternalOutput").ap()
        d["yad"] = nc.dram_tensor("yad_s", [NTSB, 128, TSB // 128, D], F32,
                                  kind="ExternalOutput").ap()
        d["scrd"] = nc.dram_tensor("scrd_s", [NTSB, 128, 128], F32,
                                   kind="ExternalOutput").ap()

    with ExitStack() as st:
        P = Prog(nc, st)
        g = {}
        g["chV"] = [st.enter_context(nc.semaphore("chV")), 0]
        g["chS"] = [st.enter_context(nc.semaphore("chS")), 0]
        g["ones"] = st.enter_context(nc.sbuf_tensor("g_ones", [1, 128], F32))
        g["c0neg"] = st.enter_context(nc.sbuf_tensor("g_c0neg", [128, 1], F32))
        g["qnmax"] = st.enter_context(nc.sbuf_tensor("g_qnmax", [16, 1], F32))
        g["knmax"] = st.enter_context(nc.sbuf_tensor("g_knmax", [16, 1], F32))
        g["ind"] = st.enter_context(nc.sbuf_tensor("g_ind", [128, NCH, 16],
                                                   BF16))
        g["r32"] = st.enter_context(nc.sbuf_tensor("g_r32", [1, 32], F32))
        g["t11"] = st.enter_context(nc.sbuf_tensor("g_t11", [1, 4], F32))

        P.op("vector", lambda e: e.memset(g["ones"][:], 1.0), inc="init")
        P.op("vector", lambda e: e.memset(g["ind"][:], 0.0), inc="init")
        for co in range(NCH):
            P.op("vector", lambda e, co=co: e.memset(
                g["ind"][0:64, co, 2 * co:2 * co + 1], 1.0), inc="init")
            P.op("vector", lambda e, co=co: e.memset(
                g["ind"][64:128, co, 2 * co + 1:2 * co + 2], 1.0), inc="init")
        P.op("vector", lambda e: e.memset(g["qnmax"][:], 0.0), inc="init")
        P.op("vector", lambda e: e.memset(g["knmax"][:], 0.0), inc="init")

        with ExitStack() as ab:
            vaug = ab.enter_context(
                nc.sbuf_tensor("vaug", [128, NT128, H, HD + 1], BF16))
            P.op("vector", lambda e: e.memset(vaug[:], 1.0), inc="init")
            with ExitStack() as a_st:
                emit_phase_A(nc, P, a_st, d, g, vaug)
            P.barrier()
            if dev_phase >= 2:
                with ExitStack() as b_st:
                    emit_phase_B(nc, P, b_st, d, g, vaug)
                P.barrier()

        if dev_phase >= 3:
            with ExitStack() as fs:
                emit_phase_F(nc, P, fs, d, g)
        elif dev_phase == 2:
            bufo = st.enter_context(nc.sbuf_tensor("bufo", [128, D], F32))
            for i in range(NT128):
                P.op("sync", lambda e, i=i: e.dma_start(
                    bufo[:], d["y1f"][i * 128:(i + 1) * 128, :]),
                    waits=["y1st0", "y1st1", ("outst", None)],
                    inc="devld", dma=True)
                P.op("sync", lambda e, i=i: e.dma_start(
                    d["out"][i * 128:(i + 1) * 128, :], bufo[:]),
                    waits=["devld"], inc="outst", dma=True)
        else:
            bufo = st.enter_context(nc.sbuf_tensor("bufo", [128, D], F32))
            P.op("vector", lambda e: e.memset(bufo[:], 0.0), inc="z")
            for i in range(NT128):
                P.op("sync", lambda e, i=i: e.dma_start(
                    d["out"][i * 128:(i + 1) * 128, :], bufo[:]),
                    waits=["z"], inc="outst", dma=True)

        P.barrier()
        with nc.Block() as block:
            P.emit(block)

    print(f"[build] ops per engine: {P.n_ops} "
          f"total={sum(P.n_ops.values())} sems={len(P.sems)}")
    return nc


ab_dbg = []


def emit_phase_A(nc, P, ab, d, g, vaug):
    sbA = lambda name, shape, dt=BF16: ab.enter_context(
        nc.sbuf_tensor(name, shape, dt))
    psA = lambda name, shape: ab.enter_context(
        nc.psum_tensor(name, shape, F32))

    w_wq = sbA("w_wq", [128, NCH, D])
    w_wk = sbA("w_wk", [128, NCH, D])
    w_wv = sbA("w_wv", [128, NCH, D])
    w_wd = sbA("w_wd", [128, NCH, DL])
    w_uk = sbA("w_uk", [128, 2, D])
    w_uv = sbA("w_uv", [128, 2, D])
    xbb = [sbA(f"xbb{p}", [128, 4, D]) for p in range(2)]
    xfm = [sbA(f"xfm{p}", [128, NCH, 512]) for p in range(2)]
    kv_sb = sbA("kv_sb", [128, NCH, 512])
    lat_sb = sbA("lat_sb", [128, 2, 512])
    q_sb = sbA("q_sb", [128, NCH, 512])
    k_sb = sbA("k_sb", [128, NCH, 512])
    sqq = [sbA(f"sqq{p}", [128, 512]) for p in range(2)]
    tmx = sbA("tmx", [16, 1], F32)
    c0one = sbA("c0one", [1, 1], F32)

    ab_dbg.clear()
    if "dbg" in d:
        ab_dbg.append(sbA("dbg_sb", [64, 512], F32))
    pQ = psA("pQ", [128, 512])
    pK = psA("pK", [128, 512])
    pV = psA("pV", [128, 512])
    pVT = psA("pVT", [128, 1024])
    pNQ = psA("pNQ", [16, 512])
    pNK = psA("pNK", [16, 512])
    pC0 = psA("pC0", [128, 1])

    for dst, src in [(w_wq, d["wq"]), (w_wk, d["wk"]), (w_wv, d["wv"]),
                     (w_wd, d["wdown"]), (w_uk, d["wupk"]), (w_uv, d["wupv"])]:
        P.op("gpsimd", lambda e, dst=dst, src=src: e.dma_start(
            dst[:], src.rearrange("(c p) o -> p c o", p=128)),
            inc="wA", dma=True)

    xfm_guard = {0: 0, 1: 0}
    for tb in range(8):
        t0 = tb * 512
        par = tb % 2
        P.op("gpsimd", lambda e, t0=t0, par=par: e.dma_start(
            xbb[par][:],
            d["x"][t0:t0 + 512, :].rearrange("(i p) f -> p i f", p=128)),
            waits=[(f"xst{par}", None)], inc=f"xcast{par}", dma=True)
        P.op("sync", lambda e, t0=t0, par=par: e.dma_start(
            d["xbf"][t0:t0 + 512, :].rearrange("(i p) f -> p i f", p=128),
            xbb[par][:]),
            waits=[f"xcast{par}"], inc=f"xst{par}", dma=True)
        P.op("scalar", lambda e, t0=t0, par=par: e.dma_start(
            xfm[par][:], d["xbf"][t0:t0 + 512, :], transpose=True),
            waits=[f"xst{par}", ("pe_v", xfm_guard[par])],
            inc=f"xfm{par}", dma=True)

        for co in range(NCH):
            def qchain(e, co=co, par=par):
                r = None
                for ci in range(NCH):
                    r = e.matmul(pQ[:], w_wq[:, ci, co * 128:(co + 1) * 128],
                                 xfm[par][:, ci, :],
                                 start=(ci == 0), stop=(ci == NCH - 1))
                return r
            P.op("tensor", qchain,
                 waits=["wA", f"xfm{par}", ("evq", None), ("latcp", None)],
                 inc="pe_q")
            P.op("vector", lambda e, co=co: e.tensor_scalar(
                q_sb[:, co, :], pQ[:], INV_SQRT_D, None, OP.mult),
                waits=[("pe_q", None), ("qst", None), ("sq_q", None)],
                inc="evq")
            sp = co % 2
            P.op("scalar", lambda e, co=co, sp=sp: e.activation(
                sqq[sp][:], q_sb[:, co, :], AF.Square),
                waits=[("evq", None), ("pe_nq", None), ("pe_nk", None)],
                inc="sq_q")
            P.op("tensor", lambda e, co=co, sp=sp: e.matmul(
                pNQ[:], g["ind"][:, co, :], sqq[sp][:],
                start=(co == 0), stop=(co == NCH - 1)),
                waits=[("sq_q", None), ("nqred", None), "init"], inc="pe_nq")

            def kchain(e, co=co, par=par):
                r = None
                for ci in range(NCH):
                    r = e.matmul(pK[:], w_wk[:, ci, co * 128:(co + 1) * 128],
                                 xfm[par][:, ci, :],
                                 start=(ci == 0), stop=(ci == NCH - 1))
                return r
            P.op("tensor", kchain, waits=[("kcp", None), ("krcp", None)],
                 inc="pe_k")
            P.op("scalar", lambda e, co=co: e.copy(kv_sb[:, co, :], pK[:]),
                 waits=[("pe_k", None), ("pe_lat", None)], inc="kcp")

            def vchain(e, co=co, par=par):
                r = None
                for ci in range(NCH):
                    r = e.matmul(pV[:], w_wv[:, ci, co * 128:(co + 1) * 128],
                                 xfm[par][:, ci, :],
                                 start=(ci == 0), stop=(ci == NCH - 1))
                return r
            P.op("tensor", vchain, waits=[("kvadd", None)], inc="pe_v")
            P.op("vector", lambda e, co=co: e.tensor_tensor(
                kv_sb[:, co, :], pV[:], kv_sb[:, co, :], OP.add),
                waits=[("pe_v", None), ("kcp", None)], inc="kvadd")
        xfm_guard[par] = P.count("pe_v")

        for lco in range(2):
            def latchain(e, lco=lco):
                r = None
                for ci in range(NCH):
                    r = e.matmul(pQ[:], w_wd[:, ci, lco * 128:(lco + 1) * 128],
                                 kv_sb[:, ci, :],
                                 start=(ci == 0), stop=(ci == NCH - 1))
                return r
            P.op("tensor", latchain,
                 waits=[("kvadd", None), ("latcp", None), ("evq", None)],
                 inc="pe_lat")
            P.op("scalar", lambda e, lco=lco: e.copy(lat_sb[:, lco, :], pQ[:]),
                 waits=[("pe_lat", None), ("pe_krec", None), ("pe_vt", None)],
                 inc="latcp")

        for co in range(NCH):
            def krchain(e, co=co):
                r = None
                for ci in range(2):
                    r = e.matmul(pK[:], w_uk[:, ci, co * 128:(co + 1) * 128],
                                 lat_sb[:, ci, :],
                                 start=(ci == 0), stop=(ci == 1))
                return r
            P.op("tensor", krchain,
                 waits=[("latcp", None), ("krcp", None), ("kcp", None)],
                 inc="pe_krec")
            P.op("scalar", lambda e, co=co: e.copy(k_sb[:, co, :], pK[:]),
                 waits=[("pe_krec", None), ("kst", None), ("sq_k", None)],
                 inc="krcp")
            sp = co % 2
            P.op("scalar", lambda e, co=co, sp=sp: e.activation(
                sqq[sp][:], k_sb[:, co, :], AF.Square),
                waits=[("krcp", None), ("pe_nk", None), ("pe_nq", None)],
                inc="sq_k")
            P.op("tensor", lambda e, co=co, sp=sp: e.matmul(
                pNK[:], g["ind"][:, co, :], sqq[sp][:],
                start=(co == 0), stop=(co == NCH - 1)),
                waits=[("sq_k", None), ("nkred", None)], inc="pe_nk")

        for i in range(4):
            def vtchain(e, i=i):
                r = None
                for n in range(2):
                    for ci in range(2):
                        r = e.matmul(
                            pVT[:, n * 512:(n + 1) * 512],
                            lat_sb[:, ci, i * 128:(i + 1) * 128],
                            w_uv[:, ci, n * 512:(n + 1) * 512],
                            start=(ci == 0), stop=(ci == 1))
                return r
            P.op("tensor", vtchain, waits=[("latcp", None), ("vev", None)],
                 inc="pe_vt")
            gi = tb * 4 + i
            P.op("vector", lambda e, gi=gi: e.tensor_copy(
                vaug[:, gi, :, 0:HD],
                pVT.ap().rearrange("p (h v) -> p h v", v=64)),
                waits=[("pe_vt", None), "init"], inc="vev")

        P.op("vector", lambda e: e.reduce_max(tmx[:], pNQ[:], axis=AX.X),
             waits=[("pe_nq", None), ("nqmax", None)], inc="nqred")
        P.op("vector", lambda e: e.tensor_tensor(
            g["qnmax"][:], tmx[:], g["qnmax"][:], OP.max),
            waits=["init", ("nqred", None), ("nqmax", None)], inc="nqmax")
        P.op("vector", lambda e: e.reduce_max(tmx[:], pNK[:], axis=AX.X),
             waits=[("pe_nk", None), ("nqmax", None), ("nkmax", None)],
             inc="nkred")
        P.op("vector", lambda e: e.tensor_tensor(
            g["knmax"][:], tmx[:], g["knmax"][:], OP.max),
            waits=[("nkred", None), ("nkmax", None)], inc="nkmax")

        P.op("sync", lambda e, t0=t0: e.dma_start(
            d["qf"].rearrange("(c p) t -> p c t", p=128)[:, :, t0:t0 + 512],
            q_sb[:]), waits=[("evq", None)], inc="qst", dma=True)
        P.op("sync", lambda e, t0=t0: e.dma_start(
            d["kf"].rearrange("(c p) t -> p c t", p=128)[:, :, t0:t0 + 512],
            k_sb[:]), waits=[("krcp", None)], inc="kst", dma=True)
        if "nrm" in d:
            P.op("sync", lambda e, tb=tb: e.dma_start(
                d["nrm"][tb, 0, :], g["qnmax"][:]),
                waits=[("nqmax", None)], inc="nrmst", dma=True)
            P.op("sync", lambda e, tb=tb: e.dma_start(
                d["nrm"][tb, 1, :], g["knmax"][:]),
                waits=[("nkmax", None)], inc="nrmst", dma=True)
        if "dbg" in d and tb == 0:
            dbg = ab_dbg[0]
            P.op("vector", lambda e: e.tensor_copy(dbg[0:16, :], pNQ[:]),
                 waits=[("pe_nq", None)], inc="dbgv")
            P.op("vector", lambda e: e.tensor_copy(dbg[32:48, :], pNK[:]),
                 waits=[("pe_nk", None)], inc="dbgv")
            P.op("sync", lambda e: e.dma_start(d["dbg"][0:16, :],
                                               dbg[0:16, :]),
                 waits=["dbgv"], inc="dbgst", dma=True)
            P.op("sync", lambda e: e.dma_start(d["dbg"][16:32, :],
                                               dbg[32:48, :]),
                 waits=["dbgv"], inc="dbgst", dma=True)

    P.op("sync", lambda e: e.dma_start(d["c0b"][0:16], g["qnmax"][:]),
         waits=[("nqmax", None)], inc="c0st", dma=True)
    P.op("sync", lambda e: e.dma_start(d["c0b"][16:32], g["knmax"][:]),
         waits=[("nkmax", None)], inc="c0st", dma=True)
    P.op("sync", lambda e: e.dma_start(
        g["r32"][:], d["c0b"].rearrange("(o a) -> o a", o=1)),
        waits=["c0st"], inc="c0ld", dma=True)
    P.op("vector", lambda e: e.reduce_max(
        g["t11"][:, 0:1], g["r32"][:, 0:16], axis=AX.X),
        waits=["c0ld"], inc="c0v")
    P.op("vector", lambda e: e.reduce_max(
        g["t11"][:, 1:2], g["r32"][:, 16:32], axis=AX.X), inc="c0v")
    P.op("vector", lambda e: e.tensor_tensor(
        g["t11"][:, 2:3], g["t11"][:, 0:1], g["t11"][:, 1:2], OP.mult),
        waits=[("c0v", None)], inc="c0v")
    P.op("scalar", lambda e: e.activation(
        g["t11"][:, 3:4], g["t11"][:, 2:3], AF.Ln),
        waits=[("c0v", None)], inc="c0a")
    P.op("scalar", lambda e: e.activation(
        c0one[:], g["t11"][:, 3:4], AF.Exp, scale=0.5),
        waits=[("c0a", None)], inc="c0a")
    P.op("vector", lambda e: e.tensor_scalar(
        c0one[:], c0one[:], -1.0, None, OP.mult),
        waits=[("c0a", None)], inc="c0n")
    P.op("tensor", lambda e: e.matmul(
        pC0[:], g["ones"][:, 0:128], c0one[:], start=True, stop=True),
        waits=[("c0n", None)], inc="c0mm")
    P.op("scalar", lambda e: e.copy(g["c0neg"][:], pC0[:]),
         waits=[("c0mm", None)], inc="c0done")


def emit_phase_B(nc, P, ab, d, g, vaug):
    sbB = lambda name, shape, dt=BF16: ab.enter_context(
        nc.sbuf_tensor(name, shape, dt))
    psB = lambda name, shape: ab.enter_context(
        nc.psum_tensor(name, shape, F32))

    w_wo = sbB("w_wo", [128, NCH, D])
    Ksb = sbB("Ksb", [128, NCH, S])
    Qsb = sbB("Qsb", [128, NCH, 512])
    dec = sbB("dec", [128, 16, 512], F32)
    esb4 = sbB("esb4", [128, 2, 2, 512], F32)
    PT8 = sbB("PT8", [128, 8, 512])
    _l = sbB("l_sb", [1, 512], F32)
    _r = sbB("r_sb", [1, 512], F32)
    l_sb = [_l, _l]
    r_sb = [_r, _r]
    _rb = sbB("rb_sb", [64, 512], F32)
    rb_sb = [_rb, _rb]
    o_all = sbB("o_all", [128, NCH, 512])
    xres = [sbB(f"xres{p}", [128, D], F32) for p in range(2)]
    _xa = sbB("xa", [128, D], F32)
    xa = [_xa, _xa]
    _y1fb = sbB("y1fb", [128, D], F32)
    y1fb = [_y1fb, _y1fb]
    y1bb = [sbB(f"y1bb{p}", [128, D]) for p in range(2)]
    stat = [sbB(f"stat{p}", [128, 8], F32) for p in range(2)]

    pSq = [psB(f"pSq{p}", [128, 512]) for p in range(4)]
    pO = [psB(f"pO{p}", [HD + 1, 512]) for p in range(2)]
    pAO = psB("pAO", [128, 1024])

    P.op("gpsimd", lambda e: e.dma_start(
        w_wo[:], d["wo"].rearrange("(c p) o -> p c o", p=128)),
        inc="wold", dma=True)

    gd = {"tt": [0, 0], "ex": [0, 0], "pv_pt": [0, 0],
          "lcp": [0, 0], "oev": [0, 0], "rbc": [0, 0]}

    for b in range(B_PER_CORE):
        P.op("sync", lambda e, b=b: e.dma_start(
            Ksb[:],
            d["kf"][:, b * S:(b + 1) * S].rearrange("(c p) t -> p c t",
                                                    p=128)),
            waits=[("pe_sc", None)], inc="Kld", dma=True)
        for qh in range(4):
            q0 = b * S + qh * 512
            P.op("sync", lambda e, q0=q0: e.dma_start(
                Qsb[:],
                d["qf"][:, q0:q0 + 512].rearrange("(c p) t -> p c t", p=128)),
                waits=[("pe_sc", None)], inc="Qld", dma=True)
            P.op("sync", lambda e, qh=qh: e.dma_start(
                dec[:],
                d["decay"][:, qh * 512:(qh + 1) * 512]
                .rearrange("(c p) q -> p c q", p=128)),
                waits=[("tt", None)], inc="decld", dma=True)
            for i in range(2):
                pr = i % 2
                P.op("sync", lambda e, q0=q0, i=i, pr=pr: e.dma_start(
                    xres[pr][:], d["x"][q0 + i * 128:q0 + (i + 1) * 128, :]),
                    waits=[(f"lnadd{pr}", None)], inc=f"xres{pr}", dma=True)

            cnt = {"tt": {}, "ex": {}, "pv": {}}

            def emit_pv(s, hh):
                hp, c = s // 16, s % 16
                h = 2 * hp + hh
                po = hh
                gi = b * 16
                w = [("ex", cnt["ex"][(s, hh)])]
                if c == 0:
                    w += [("lcp", gd["lcp"][po]), ("oev", gd["oev"][po])]
                sl = ((s // 2) % 2) * 4 + hh * 2 + (s % 2)
                P.op("tensor", lambda e, gi=gi, h=h, c=c, po=po, sl=sl:
                     e.matmul(pO[po][:], vaug[:, gi + c, h, :],
                              PT8[:, sl, :], start=(c == 0), stop=(c == 15)),
                     waits=w, inc="pe_pv")
                cnt["pv"][(s, hh)] = P.count("pe_pv")
                if c == 15:
                    emit_evac(h, po)

            def emit_evac(h, po):
                ch2, hb2 = h // 2, (h % 2) * 64
                P.op("scalar", lambda e, po=po: e.copy(
                    l_sb[po][:], pO[po][64:65, :]),
                    waits=[("pe_pv", None)], inc="lcp")
                P.op("scalar", lambda e, po=po: e.activation(
                    r_sb[po][:], l_sb[po][:], AF.Ln),
                    waits=[("lcp", None), ("pe_rb", None)], inc="lcp")
                P.op("scalar", lambda e, po=po: e.activation(
                    r_sb[po][:], r_sb[po][:], AF.Exp, scale=-1.0),
                    waits=[("lcp", None)], inc="lcp")
                gd["lcp"][po] = P.count("lcp")
                P.op("tensor", lambda e, po=po: e.matmul(
                    pAO[0:64, 0:512], g["ones"][:, 0:64], r_sb[po][:],
                    start=True, stop=True),
                    waits=[("lcp", None), ("rbc", None),
                           ("lnadd0", None), ("lnadd1", None)], inc="pe_rb")
                P.op("vector", lambda e, po=po: e.tensor_copy(
                    rb_sb[po][:], pAO[0:64, 0:512]),
                    waits=[("pe_rb", None), ("oev", None)], inc="rbc")
                gd["rbc"][po] = P.count("rbc")
                P.op("vector",
                     lambda e, ch2=ch2, hb2=hb2, po=po: e.tensor_tensor(
                         o_all[hb2:hb2 + 64, ch2, :], pO[po][0:64, :],
                         rb_sb[po][:], OP.mult),
                     waits=[("rbc", None), ("pe_wo", None)], inc="oev")
                gd["oev"][po] = P.count("oev")

            for s in range(128):
                hp, c = s // 16, s % 16
                cp = c % 2
                p2 = s // 2

                def smm2(e, hp=hp, c=c, cp=cp):
                    e.matmul(pSq[cp][:], Ksb[0:64, hp, c * 128:c * 128 + 128],
                             Qsb[0:64, hp, :], start=True, stop=True)
                    return e.matmul(pSq[2 + cp][:],
                                    Ksb[64:128, hp, c * 128:c * 128 + 128],
                                    Qsb[64:128, hp, :],
                                    start=True, stop=True)
                P.op("tensor", smm2,
                     waits=["Kld", "Qld", "c0done",
                            ("tt", cnt["tt"].get((s - 2, 1), 0))],
                     inc="pe_sc")
                for hh in range(2):
                    P.op("vector", lambda e, c=c, cp=cp, hh=hh:
                         e.tensor_tensor(esb4[:, hh, cp, :],
                                         pSq[hh * 2 + cp][:],
                                         dec[:, c, :], OP.mult),
                         waits=[("pe_sc", None), "decld",
                                ("ex", cnt["ex"].get((s - 2, hh), 0))],
                         inc="tt")
                    cnt["tt"][(s, hh)] = P.count("tt")
                if cp == 1:
                    for hh in range(2):
                        sl = (p2 % 2) * 4 + hh * 2
                        P.op("scalar", lambda e, hh=hh, sl=sl:
                             e.activation(PT8[:, sl:sl + 2, :],
                                          esb4[:, hh, :, :], AF.Exp,
                                          bias=g["c0neg"][:], scale=1.0),
                             waits=[("tt", None),
                                    ("pe_pv",
                                     cnt["pv"].get((s - 4, hh), 0))],
                             inc="ex")
                        cnt["ex"][(s - 1, hh)] = P.count("ex")
                        cnt["ex"][(s, hh)] = P.count("ex")
                if s >= 2 and s % 2 == 1:
                    for ss in (s - 3, s - 2):
                        emit_pv(ss, 0)
                        emit_pv(ss, 1)
            for s in (126, 127):
                emit_pv(s, 0)
                emit_pv(s, 1)

            for i in range(4):
                pr = i % 2
                tg = q0 + i * 128

                def wochain(e, i=i):
                    r = None
                    for n in range(2):
                        for ci in range(NCH):
                            r = e.matmul(
                                pAO[:, n * 512:(n + 1) * 512],
                                o_all[:, ci, i * 128:(i + 1) * 128],
                                w_wo[:, ci, n * 512:(n + 1) * 512],
                                start=(ci == 0), stop=(ci == NCH - 1))
                    return r
                P.op("tensor", wochain,
                     waits=[("oev", None), "wold", ("rbc", None),
                            ("lnadd0", None), ("lnadd1", None)], inc="pe_wo")
                P.op("vector", lambda e, pr=pr: e.tensor_tensor(
                    xa[pr][:], pAO[:], xres[pr][:], OP.add),
                    waits=[("pe_wo", None), (f"xres{pr}", None),
                           ("y1w0", None), ("y1w1", None), ("lnsq", None),
                           ("rstd0", None), ("rstd1", None)],
                    inc=f"lnadd{pr}")
                P.op("vector", lambda e, pr=pr: e.reduce_sum(
                    stat[pr][:, 0:1], xa[pr][:], axis=AX.X),
                    waits=[(f"lnadd{pr}", None)], inc=f"lnst{pr}")
                P.op("scalar", lambda e, pr=pr: e.activation(
                    y1bb[pr][:], xa[pr][:], AF.Square,
                    accum_out=stat[pr][:, 1:2]),
                    waits=[(f"lnadd{pr}", None), ("lnsq", None),
                           (f"y1st{pr}", None)], inc="lnsq")

                def lnchain(e, pr=pr):
                    ch = g["chV"]
                    cstep(e, ch, e.tensor_scalar(
                        stat[pr][:, 2:3], stat[pr][:, 0:1], 1.0 / D, None,
                        OP.mult))
                    cstep(e, ch, e.tensor_scalar(
                        stat[pr][:, 3:4], stat[pr][:, 1:2], 1.0 / D, None,
                        OP.mult))
                    cstep(e, ch, e.tensor_mul(
                        stat[pr][:, 4:5], stat[pr][:, 2:3],
                        stat[pr][:, 2:3]))
                    cstep(e, ch, e.tensor_sub(
                        stat[pr][:, 5:6], stat[pr][:, 3:4],
                        stat[pr][:, 4:5]))
                    return e.tensor_scalar(stat[pr][:, 5:6],
                                           stat[pr][:, 5:6], EPS, None,
                                           OP.add)
                P.op("vector", lnchain,
                     waits=[("lnsq", None), (f"lnst{pr}", None)],
                     inc=f"lnst{pr}")

                def rstdchain(e, pr=pr):
                    cstep(e, g["chS"], e.activation(
                        stat[pr][:, 6:7], stat[pr][:, 5:6], AF.Ln))
                    return e.activation(stat[pr][:, 7:8], stat[pr][:, 6:7],
                                        AF.Exp, scale=-0.5)
                P.op("scalar", rstdchain, waits=[(f"lnst{pr}", None)],
                     inc=f"rstd{pr}")
                P.op("vector", lambda e, pr=pr: e.tensor_scalar(
                    y1fb[pr][:], xa[pr][:], stat[pr][:, 2:3],
                    stat[pr][:, 7:8], OP.subtract, OP.mult),
                    waits=[(f"rstd{pr}", None), ("y1st0", None),
                           ("y1st1", None)],
                    inc=f"y1w{pr}")
                P.op("vector", lambda e, pr=pr: e.tensor_scalar(
                    y1bb[pr][:], xa[pr][:], stat[pr][:, 2:3],
                    stat[pr][:, 7:8], OP.subtract, OP.mult),
                    inc=f"y1w{pr}")
                P.op("sync", lambda e, tg=tg, pr=pr: e.dma_start(
                    d["y1f"][tg:tg + 128, :], y1fb[pr][:]),
                    waits=[(f"y1w{pr}", None)], inc=f"y1st{pr}", dma=True)
                P.op("sync", lambda e, tg=tg, pr=pr: e.dma_start(
                    d["y1b"][tg:tg + 128, :], y1bb[pr][:]),
                    inc=f"y1st{pr}", dma=True)
                if i + 2 < 4:
                    P.op("sync", lambda e, q0=q0, i=i, pr=pr: e.dma_start(
                        xres[pr][:],
                        d["x"][q0 + (i + 2) * 128:q0 + (i + 3) * 128, :]),
                        waits=[(f"lnadd{pr}", None)], inc=f"xres{pr}",
                        dma=True)


def emit_phase_F(nc, P, fs, d, g):
    sbF = lambda name, shape, dt=BF16: fs.enter_context(
        nc.sbuf_tensor(name, shape, dt))
    psF = lambda name, shape: fs.enter_context(
        nc.psum_tensor(name, shape, F32))

    ring = [sbF(f"ring{s}", [128, NCH, 512]) for s in range(4)]
    dpb = [sbF(f"dpb{p}", [128, NI, D]) for p in range(2)]
    sg_sb = sbF("sg_sb", [128, NI, TSB])
    act_sb = sbF("act_sb", [128, NI, TSB])
    y1fm = [sbF(f"y1fm{p}", [128, NCH, TSB]) for p in range(2)]
    yacc = sbF("yacc", [128, TSB // 128, D], F32)
    gw_sb = sbF("gw_sb", [128, NCH, E])
    gb_sb = sbF("gb_sb", [1, E], F32)
    tn_sb = sbF("tn_sb", [1, E], F32)
    gbb = sbF("gbb", [128, E], F32)
    tnb = sbF("tnb", [128, E], F32)
    scr = [sbF(f"scr{p}", [128, 128], F32) for p in range(2)]
    cw = [sbF(f"cw{p}", [128, TSB // 128, E], F32) for p in range(2)]
    gwk = sbF("gwk", [128, 96], F32)
    y1tm = [sbF(f"y1tm{p}", [128, D], F32) for p in range(2)]
    outb = [sbF(f"outb{p}", [128, D], F32) for p in range(2)]
    sqd = sbF("sqdF", [128, D])
    stA = sbF("stA", [128, 8], F32)
    stB = sbF("stB", [128, 8], F32)
    stC = sbF("stC", [128, 8], F32)

    pG = [psF(f"pG{p}", [128, 512]) for p in range(2)]
    pU = [psF(f"pU{p}", [128, 512]) for p in range(2)]
    pY = [psF(f"pY{p}", [128, 1024]) for p in range(2)]

    NT = TSB // 128

    P.op("gpsimd", lambda e: e.dma_start(
        gw_sb[:], d["gwT"].rearrange("(c p) o -> p c o", p=128)),
        inc="gwld", dma=True)
    P.op("sync", lambda e: e.dma_start(gb_sb[:], d["gbias"][:, :]),
         inc="gbld", dma=True)
    P.op("sync", lambda e: e.dma_start(tn_sb[:], d["tinyc"][:, :]),
         inc="gbld", dma=True)
    P.op("tensor", lambda e: e.matmul(pG[0][:, 0:16], g["ones"][:, 0:128],
                                      gb_sb[:], start=True, stop=True),
         waits=["gbld"], inc="pe_gc")
    P.op("tensor", lambda e: e.matmul(pG[1][:, 0:16], g["ones"][:, 0:128],
                                      tn_sb[:], start=True, stop=True),
         inc="pe_gc")
    P.op("scalar", lambda e: e.copy(gbb[:], pG[0][:, 0:16]),
         waits=[("pe_gc", None)], inc="gc")
    P.op("scalar", lambda e: e.copy(tnb[:], pG[1][:, 0:16]), inc="gc")

    slot_state = [0] * 4
    next_slot = [0]
    dp_guard = {0: 0, 1: 0}
    uact_names = ("uact0", "uact1")
    pairs = [(i, j) for i in range(4) for j in range(i + 1, 4)]

    def load_ring(src_ap):
        s = next_slot[0] % 4
        next_slot[0] += 1
        guard = slot_state[s]
        w = src_ap.shape[-1]
        P.op("gpsimd", lambda e, s=s, w=w, src_ap=src_ap: e.dma_start(
            ring[s][:, :, 0:w], src_ap),
            waits=[("pe_gu", guard)], inc=f"ring{s}", dma=True)
        return s

    gate_state = [None]

    def gchain_emit(i8, par):
        def gchain(e, i8=i8, par=par):
            ch = g["chV"]
            cs = lambda inst: cstep(e, ch, inst)
            sc = scr[par][:, i8 * 16:(i8 + 1) * 16]
            sfc = gwk[:, 0:16]
            cs(e.tensor_tensor(sfc, sc, gbb[:], OP.add))
            sfg = sfc.rearrange("p (gg m) -> p gg m", m=4)
            for k, (i, j) in enumerate(pairs):
                cs(e.tensor_tensor(gwk[:, 24 + k * 4:28 + k * 4],
                                   sfg[:, :, i], sfg[:, :, j], OP.add))
            gs = gwk[:, 16:20]
            cs(e.tensor_tensor(gs, gwk[:, 24:28], gwk[:, 28:32], OP.max))
            for k in range(2, 6):
                cs(e.tensor_tensor(gs, gs, gwk[:, 24 + k * 4:28 + k * 4],
                                   OP.max))
            for k, (i, j) in enumerate(pairs):
                cs(e.tensor_tensor(gwk[:, 48 + k:49 + k], gs[:, i:i + 1],
                                   gs[:, j:j + 1], OP.min))
            cs(e.tensor_tensor(gwk[:, 54:55], gwk[:, 48:49], gwk[:, 49:50],
                               OP.max))
            for k in range(2, 6):
                cs(e.tensor_tensor(gwk[:, 54:55], gwk[:, 54:55],
                                   gwk[:, 48 + k:49 + k], OP.max))
            gmask = gwk[:, 20:24]
            cs(e.tensor_scalar(gmask, gs, gwk[:, 54:55], None, OP.is_ge))
            gmb = bass.AP(gmask.tensor, gmask.offset,
                          [gmask.ap[0], [1, 4], [0, 4]])
            tmp = gwk[:, 32:48]
            cs(e.tensor_tensor(tmp, sfc, tnb[:], OP.subtract))
            cs(e.tensor_tensor(tmp, tmp, gmb, OP.mult))
            cs(e.tensor_tensor(tmp, tmp, tnb[:], OP.add))
            w = gwk[:, 0:16]
            cs(e.tensor_copy(w, tmp))
            msk = gwk[:, 64:80]
            mx = gwk[:, 80:81]
            for _ in range(3):
                cs(e.reduce_max(mx, w, axis=AX.X))
                cs(e.tensor_scalar(msk, w, mx, None, OP.is_ge))
                cs(e.scalar_tensor_tensor(w, msk, -1e38, w, OP.mult,
                                          OP.add))
            cs(e.reduce_max(mx, w, axis=AX.X))
            sel = gwk[:, 64:80]
            cs(e.tensor_scalar(sel, tmp, mx, None, OP.is_ge))
            wu = gwk[:, 32:48]
            cs(e.tensor_tensor(wu, sel, sc, OP.mult))
            cs(e.reduce_sum(gwk[:, 81:82], wu, axis=AX.X))
            cs(e.tensor_scalar(gwk[:, 81:82], gwk[:, 81:82], 1e-20, None,
                               OP.add))
            cs(e.reciprocal(gwk[:, 82:83], gwk[:, 81:82]))
            return e.tensor_scalar(cw[par][:, i8, :], wu, gwk[:, 82:83],
                                   None, OP.mult)
        P.op("vector", gchain, waits=[("glog", None), "gc"], inc="gdve")

    def gate_emit(tsb):
        par = tsb % 2
        t0 = tsb * TSB
        P.op("scalar", lambda e, t0=t0, par=par: e.dma_start(
            y1fm[par][:], d["y1b"][t0:t0 + TSB, :], transpose=True),
            waits=[("pe_gu", None), ("pe_glog", None), "y1st0", "y1st1"],
            inc=f"y1fm{par}", dma=True)

        def logits(e, par=par):
            r = None
            for i8 in range(NT):
                for ci in range(NCH):
                    r = e.matmul(pU[0][:, i8 * 16:(i8 + 1) * 16],
                                 y1fm[par][:, ci, i8 * 128:(i8 + 1) * 128],
                                 gw_sb[:, ci, :],
                                 start=(i8 == 0 and ci == 0),
                                 stop=(i8 == NT - 1 and ci == NCH - 1),
                                 skip_group_check=not (
                                     (i8 == 0 and ci == 0)
                                     or (i8 == NT - 1 and ci == NCH - 1)))
            return r
        P.op("tensor", logits,
             waits=[f"y1fm{par}", "gwld", "gc", ("glog", None),
                    ("uact0", None), ("uact1", None)], inc="pe_glog")
        P.op("scalar", lambda e, par=par: e.activation(
            scr[par][:], pU[0][:, 0:128], AF.Sigmoid),
            waits=[("pe_glog", None), ("gdve", None)], inc="glog")

        gate_state[0] = (tsb, par)
        if "cwd" in d:
            P.op("sync", lambda e, tsb=tsb, par=par: e.dma_start(
                d["cwd"][tsb], cw[par][:]), waits=["gdve"], inc="dvst",
                dma=True)
            P.op("sync", lambda e, tsb=tsb, par=par: e.dma_start(
                d["scrd"][tsb], scr[par][:]), waits=["glog"], inc="dvst",
                dma=True)

    def expert_emit(tsb, ee):
        par = tsb % 2
        routed = ee < E
        se = ee - E
        if routed:
            gsrc, usrc, dsrc = d["gpT"][ee], d["upT"][ee], d["dpT"][ee]
        else:
            gsrc = d["sgT"][:, se * I:(se + 1) * I]
            usrc = d["suT"][:, se * I:(se + 1) * I]
            dsrc = d["sdT"][se * I:(se + 1) * I, :]

        widths = [512, 512, 384]
        gslots = [load_ring(gsrc[:, j * 512:j * 512 + w]
                            .rearrange("(c p) i -> p c i", p=128))
                  for j, w in enumerate(widths)]
        dpar = ee % 2
        P.op("gpsimd", lambda e, dsrc=dsrc, dpar=dpar: e.dma_start(
            dpb[dpar][:], dsrc.rearrange("(c p) h -> p c h", p=128)),
            waits=[("pe_dn", dp_guard[dpar])], inc=f"dpld{dpar}",
            dma=True)

        for ic in range(NI):
            sl = gslots[ic // 4]
            off = (ic % 4) * 128
            for th in range(2):
                gp = th

                def gmm(e, ic=ic, th=th, sl=sl, off=off, gp=gp, par=par):
                    r = None
                    for ci in range(NCH):
                        r = e.matmul(
                            pG[gp][:],
                            ring[sl][:, ci, off:off + 128],
                            y1fm[par][:, ci, th * 512:(th + 1) * 512],
                            start=(ci == 0), stop=(ci == NCH - 1))
                    return r
                P.op("tensor", gmm,
                     waits=[(f"ring{sl}", None), f"y1fm{par}", "gc",
                            (f"silu{gp}", None)],
                     inc="pe_gu")
                P.op("scalar",
                     lambda e, ic=ic, th=th, gp=gp: e.activation(
                         sg_sb[:, ic, th * 512:(th + 1) * 512],
                         pG[gp][:], AF.Silu),
                     waits=[("pe_gu", None), ("uact0", None),
                            ("uact1", None)],
                     inc=f"silu{gp}")
            slot_state[sl] = P.count("pe_gu")
        uslots = [load_ring(usrc[:, j * 512:j * 512 + w]
                            .rearrange("(c p) i -> p c i", p=128))
                  for j, w in enumerate(widths)]
        for ic in range(NI):
            sl = uslots[ic // 4]
            off = (ic % 4) * 128
            for th in range(2):
                up = th

                def umm(e, ic=ic, th=th, sl=sl, off=off, up=up, par=par):
                    r = None
                    for ci in range(NCH):
                        r = e.matmul(
                            pU[up][:],
                            ring[sl][:, ci, off:off + 128],
                            y1fm[par][:, ci, th * 512:(th + 1) * 512],
                            start=(ci == 0), stop=(ci == NCH - 1))
                    return r
                P.op("tensor", umm,
                     waits=[(f"ring{sl}", None), (uact_names[up], None),
                            ("pe_glog", None), ("glog", None)],
                     inc="pe_gu")
                P.op("vector",
                     lambda e, ic=ic, th=th, up=up: e.tensor_tensor(
                         act_sb[:, ic, th * 512:(th + 1) * 512],
                         pU[up][:],
                         sg_sb[:, ic, th * 512:(th + 1) * 512],
                         OP.mult),
                     waits=[("pe_gu", None), (f"silu{up}", None),
                            ("pe_dn", None)], inc=uact_names[up])
            slot_state[sl] = P.count("pe_gu")

        if "actd" in d and tsb == 0 and ee == 0:
            P.op("sync", lambda e: e.dma_start(d["actd"][:], act_sb[:]),
                 waits=["uact0", "uact1"], inc="dvst", dma=True)
            P.op("sync", lambda e: e.dma_start(d["sgd"][:], sg_sb[:]),
                 waits=["silu0", "silu1"], inc="dvst", dma=True)
        for i8 in range(NT):
            yp = i8 % 2

            def dmm(e, i8=i8, yp=yp, dpar=dpar):
                r = None
                for ic in range(NI):
                    for n in range(2):
                        r = e.matmul(
                            pY[yp][:, n * 512:(n + 1) * 512],
                            act_sb[:, ic, i8 * 128:(i8 + 1) * 128],
                            dpb[dpar][:, ic, n * 512:(n + 1) * 512],
                            start=(ic == 0), stop=(ic == NI - 1))
                return r
            P.op("tensor", dmm,
                 waits=[("uact0", None), ("uact1", None),
                        (f"dpld{dpar}", None), (f"yadd{yp}", None)],
                 inc="pe_dn")
            if routed and ee == 0:
                P.op("vector", lambda e, i8=i8, yp=yp, par=par:
                     e.tensor_scalar(
                         yacc[:, i8, :], pY[yp][:], cw[par][:, i8, 0:1],
                         None, OP.mult),
                     waits=[("pe_dn", None), ("gdve", None),
                            ("outw", None)], inc=f"yadd{yp}")
            elif routed:
                P.op("vector",
                     lambda e, i8=i8, yp=yp, ee=ee, par=par:
                     e.scalar_tensor_tensor(
                         yacc[:, i8, :], pY[yp][:],
                         cw[par][:, i8, ee:ee + 1], yacc[:, i8, :],
                         OP.mult, OP.add),
                     waits=[("pe_dn", None)], inc=f"yadd{yp}")
            else:
                P.op("vector", lambda e, i8=i8, yp=yp: e.tensor_tensor(
                    yacc[:, i8, :], pY[yp][:], yacc[:, i8, :], OP.add),
                    waits=[("pe_dn", None)], inc=f"yadd{yp}")
        dp_guard[dpar] = P.count("pe_dn")

    def finalize_emit(tsb):
        t0 = tsb * TSB
        if "yad" in d:
            P.op("sync", lambda e, tsb=tsb: e.dma_start(
                d["yad"][tsb], yacc[:]),
                waits=["yadd0", "yadd1"], inc="dvst", dma=True)
        for i8 in range(NT):
            pr = i8 % 2
            tg = t0 + i8 * 128
            P.op("sync", lambda e, tg=tg, pr=pr: e.dma_start(
                y1tm[pr][:], d["y1f"][tg:tg + 128, :]),
                waits=[(f"fln{pr}", None)], inc=f"y1tm{pr}", dma=True)
            P.op("vector", lambda e, i8=i8, pr=pr: e.tensor_tensor(
                yacc[:, i8, :], y1tm[pr][:], yacc[:, i8, :], OP.add),
                waits=[(f"y1tm{pr}", None), ("yadd0", None),
                       ("yadd1", None)], inc=f"fln{pr}")
            P.op("vector", lambda e, i8=i8: e.reduce_sum(
                stA[:, i8:i8 + 1], yacc[:, i8, :], axis=AX.X),
                waits=[(f"fln{pr}", None)], inc="flnr")
            P.op("scalar", lambda e, i8=i8: e.activation(
                sqd[:], yacc[:, i8, :], AF.Square,
                accum_out=stB[:, i8:i8 + 1]),
                waits=[(f"fln{pr}", None), ("flnsq", None)], inc="flnsq")
        def stats(e):
            ch = g["chV"]
            cstep(e, ch, e.tensor_scalar(stA[:], stA[:], 1.0 / D, None,
                                         OP.mult))
            cstep(e, ch, e.tensor_scalar(stB[:], stB[:], 1.0 / D, None,
                                         OP.mult))
            cstep(e, ch, e.tensor_mul(stC[:], stA[:], stA[:]))
            cstep(e, ch, e.tensor_sub(stB[:], stB[:], stC[:]))
            return e.tensor_scalar(stB[:], stB[:], EPS, None, OP.add)
        P.op("vector", stats, waits=[("flnr", None), ("flnsq", None)],
             inc="flns")

        def rstd8(e):
            cstep(e, g["chS"], e.activation(stC[:], stB[:], AF.Ln))
            return e.activation(stB[:], stC[:], AF.Exp, scale=-0.5)
        P.op("scalar", rstd8, waits=[("flns", None)], inc="flna")
        for i8 in range(NT):
            pr = i8 % 2
            tg = t0 + i8 * 128
            P.op("vector", lambda e, i8=i8, pr=pr: e.tensor_scalar(
                outb[pr][:], yacc[:, i8, :], stA[:, i8:i8 + 1],
                stB[:, i8:i8 + 1], OP.subtract, OP.mult),
                waits=[("flna", None), (f"outst{pr}", None)],
                inc="outw")
            P.op("sync", lambda e, tg=tg, pr=pr: e.dma_start(
                d["out"][tg:tg + 128, :], outb[pr][:]),
                waits=[("outw", None)], inc=f"outst{pr}", dma=True)

    for tsb in range(NTSB):
        if tsb == 0:
            gate_emit(0)
            for i8 in range(NT):
                gchain_emit(i8, 0)
            gate_state[0] = None
        for ee in range(DEV_NEXPERTS):
            expert_emit(tsb, ee)
            if ee == 7 and tsb + 1 < NTSB and DEV_NEXPERTS > 9:
                gate_emit(tsb + 1)
            if ee >= 9 and gate_state[0] is not None and ee <= 16:
                gtsb, gpar = gate_state[0]
                gchain_emit(ee - 9, gpar)
                if ee == 16:
                    gate_state[0] = None
        if DEV_NEXPERTS <= 9 and tsb + 1 < NTSB:
            gate_emit(tsb + 1)
            for i8 in range(NT):
                gchain_emit(i8, (tsb + 1) % 2)
            gate_state[0] = None
        finalize_emit(tsb)


_prog_cache = {}


def _get_prog(dev_phase=3):
    key = dev_phase
    if key not in _prog_cache:
        _prog_cache[key] = build_program(dev_phase=dev_phase)
    return _prog_cache[key]


def _stage_weights(inputs):
    asc = np.ascontiguousarray
    f = np.float32

    def a(x):
        return asc(np.asarray(x, dtype=f))
    w = {
        "wq": a(inputs["L0_attn_W_q"]),
        "wk": a(inputs["L0_attn_W_k"]),
        "wv": a(inputs["L0_attn_W_v"]),
        "wo": a(inputs["L0_attn_W_o"]),
        "wdown": a(inputs["L0_attn_W_down_kv"]),
        "wupk": a(inputs["L0_attn_W_up_k"]),
        "wupv": a(inputs["L0_attn_W_up_v"]),
        "gwT": a(np.asarray(inputs["L0_ffn_gate_w"], dtype=f).T),
        "gbias": a(np.asarray(inputs["L0_ffn_gate_bias"],
                              dtype=f).reshape(1, E)),
        "gpT": a(np.asarray(inputs["L0_ffn_gate_proj"],
                            dtype=f).transpose(0, 2, 1)),
        "upT": a(np.asarray(inputs["L0_ffn_up_proj"],
                            dtype=f).transpose(0, 2, 1)),
        "dpT": a(np.asarray(inputs["L0_ffn_down_proj"],
                            dtype=f).transpose(0, 2, 1)),
        "sgT": a(np.asarray(inputs["L0_ffn_shared_gate_w"], dtype=f).T),
        "suT": a(np.asarray(inputs["L0_ffn_shared_up_w"], dtype=f).T),
        "sdT": a(np.asarray(inputs["L0_ffn_shared_down_w"], dtype=f).T),
    }
    t = np.arange(S, dtype=f)
    w["decay"] = np.exp(-LAM * np.abs(t[:, None] - t[None, :])).astype(f)
    w["tinyc"] = (-(np.arange(E, dtype=f) + 1) * 1e-30).reshape(1, E)
    return w


def kernel(**inputs):
    nc = _get_prog(3)
    w = _stage_weights(inputs)
    x = np.asarray(inputs["L0_attn_x"], dtype=np.float32)
    in_maps = []
    for c in range(NCORES):
        m = dict(w)
        m["x"] = np.ascontiguousarray(
            x[c * B_PER_CORE:(c + 1) * B_PER_CORE].reshape(T, D))
        in_maps.append(m)
    res = run_bass_kernel_spmd(nc, in_maps, core_ids=list(range(NCORES)))
    out = np.concatenate(
        [r["out"].reshape(B_PER_CORE, S, D) for r in res.results], axis=0)
    return out


# version 08
# speedup vs baseline: 1.0637x; 1.0637x over previous
import numpy as np
from contextlib import ExitStack

import concourse.bass as bass
import concourse.mybir as mybir
import concourse.bass_utils as _bass_utils
from concourse.bass_utils import run_bass_kernel_spmd
from concourse.alu_op_type import AluOpType as OP

_orig_run_command = _bass_utils.run_command


def _run_command_ldwopt(cmd, *a, **kw):
    if isinstance(cmd, list):
        cmd = ["--enable-ldw-opt=true" if c == "--enable-ldw-opt=false"
               else c for c in cmd]
    return _orig_run_command(cmd, *a, **kw)


_bass_utils.run_command = _run_command_ldwopt

F32 = mybir.dt.float32
BF16 = mybir.dt.bfloat16
AF = mybir.ActivationFunctionType
AX = mybir.AxisListType

NCORES = 8
B_PER_CORE = 2
S = 2048
D = 1024
H = 16
HD = 64
DL = 256
E = 16
I = 1408
NI = 11
IS = 2816
T = B_PER_CORE * S
NT128 = T // 128
NCH = D // 128
LAM = 0.01
EPS = 1e-5
INV_SQRT_D = 1.0 / 32.0
NTSB = 4
TSB = T // NTSB


class Prog:
    ENGINES = ("sync", "scalar", "vector", "tensor", "gpsimd")

    def __init__(self, nc, stack):
        self.nc = nc
        self.stack = stack
        self.sched = []
        self.sems = {}
        self.satisfied = {e: {} for e in self.ENGINES}
        self.n_ops = {e: 0 for e in self.ENGINES}

    def sem(self, name):
        if name not in self.sems:
            h = self.stack.enter_context(self.nc.semaphore(name))
            self.sems[name] = [h, 0]
        return self.sems[name]

    def count(self, name):
        return self.sems[name][1] if name in self.sems else 0

    def op(self, eng, fn=None, waits=(), inc=None, dma=False):
        wl = []
        for w in waits:
            name, val = (w, None) if isinstance(w, str) else w
            if name not in self.sems:
                continue
            s = self.sems[name]
            v = s[1] if val is None else val
            assert v <= s[1], f"wait {name}>={v} exceeds scheduled {s[1]}"
            if v <= 0 or self.satisfied[eng].get(name, 0) >= v:
                continue
            self.satisfied[eng][name] = v
            wl.append((s[0], v))
        incs = None
        if inc is not None:
            s = self.sem(inc)
            k = 16 if dma else 1
            s[1] += k
            incs = (s[0], k)
        self.sched.append((eng, wl, fn, incs))
        self.n_ops[eng] += 1

    def barrier(self):
        names = list(self.sems.keys())
        for e in self.ENGINES:
            self.op(e, None, waits=names)

    def emit(self, block):
        decos = {"sync": block.sync, "scalar": block.scalar,
                 "vector": block.vector, "tensor": block.tensor,
                 "gpsimd": block.gpsimd}

        def make_body(name):
            def body(eng):
                for (en, wl, fn, incs) in self.sched:
                    if en != name:
                        continue
                    for (h, v) in wl:
                        eng.wait_ge(h, v)
                    if fn is None:
                        continue
                    r = fn(eng)
                    if incs is not None:
                        r.then_inc(incs[0], incs[1])
            return body
        for name in self.ENGINES:
            decos[name](make_body(name))


def cstep(e, ch, inst):
    inst.then_inc(ch[0], 1)
    ch[1] += 1
    e.wait_ge(ch[0], ch[1])
    return inst


DEV_DUMPS = False
DEV_NEXPERTS = E + 2


def build_program(dev_phase=3):
    nc = bass.Bass("TRN2", target_bir_lowering=False, debug=False)
    dk = "ExternalOutput" if dev_phase < 3 else "Internal"

    d = {}
    d["x"] = nc.dram_tensor("x", [T, D], F32, kind="ExternalInput").ap()
    for nm, shape in [("wq", [D, D]), ("wk", [D, D]), ("wv", [D, D]),
                      ("wo", [D, D]), ("wdown", [D, DL]), ("wupk", [DL, D]),
                      ("wupv", [DL, D]), ("decay", [S, S]), ("gwT", [D, E]),
                      ("gbias", [1, E]), ("tinyc", [1, E]),
                      ("gpT", [E, D, I]), ("upT", [E, D, I]),
                      ("dpT", [E, I, D]), ("sgT", [D, IS]), ("suT", [D, IS]),
                      ("sdT", [IS, D])]:
        d[nm] = nc.dram_tensor(nm, shape, F32, kind="ExternalInput").ap()
    d["xbf"] = nc.dram_tensor("xbf_s", [T, D], BF16, kind=dk).ap()
    d["qf"] = nc.dram_tensor("qf_s", [D, T], BF16, kind=dk).ap()
    d["kf"] = nc.dram_tensor("kf_s", [D, T], BF16, kind=dk).ap()
    d["c0b"] = nc.dram_tensor("c0b_s", [32], F32, kind=dk).ap()
    if dev_phase == 1:
        d["nrm"] = nc.dram_tensor("nrm_s", [8, 2, 16], F32,
                                  kind="ExternalOutput").ap()
        d["dbg"] = nc.dram_tensor("dbg_s", [32, 512], F32,
                                  kind="ExternalOutput").ap()
    yk = dk if dev_phase == 2 else ("Internal" if dev_phase >= 3 else dk)
    d["y1b"] = nc.dram_tensor("y1b_s", [T, D], BF16, kind=yk).ap()
    d["y1f"] = nc.dram_tensor("y1f_s", [T, D], F32, kind=yk).ap()
    d["out"] = nc.dram_tensor("out", [T, D], F32, kind="ExternalOutput").ap()
    if DEV_DUMPS and dev_phase >= 3:
        d["actd"] = nc.dram_tensor("actd_s", [128, NI, TSB], BF16,
                                   kind="ExternalOutput").ap()
        d["sgd"] = nc.dram_tensor("sgd_s", [128, NI, TSB], BF16,
                                  kind="ExternalOutput").ap()
        d["cwd"] = nc.dram_tensor("cwd_s", [NTSB, 128, TSB // 128, E], F32,
                                  kind="ExternalOutput").ap()
        d["yad"] = nc.dram_tensor("yad_s", [NTSB, 128, TSB // 128, D], F32,
                                  kind="ExternalOutput").ap()
        d["scrd"] = nc.dram_tensor("scrd_s", [NTSB, 128, 128], F32,
                                   kind="ExternalOutput").ap()

    with ExitStack() as st:
        P = Prog(nc, st)
        g = {}
        g["chV"] = [st.enter_context(nc.semaphore("chV")), 0]
        g["chS"] = [st.enter_context(nc.semaphore("chS")), 0]
        g["ones"] = st.enter_context(nc.sbuf_tensor("g_ones", [1, 128], F32))
        g["c0neg"] = st.enter_context(nc.sbuf_tensor("g_c0neg", [128, 1], F32))
        g["qnmax"] = st.enter_context(nc.sbuf_tensor("g_qnmax", [16, 1], F32))
        g["knmax"] = st.enter_context(nc.sbuf_tensor("g_knmax", [16, 1], F32))
        g["ind"] = st.enter_context(nc.sbuf_tensor("g_ind", [128, NCH, 16],
                                                   BF16))
        g["r32"] = st.enter_context(nc.sbuf_tensor("g_r32", [1, 32], F32))
        g["t11"] = st.enter_context(nc.sbuf_tensor("g_t11", [1, 4], F32))

        P.op("vector", lambda e: e.memset(g["ones"][:], 1.0), inc="init")
        P.op("vector", lambda e: e.memset(g["ind"][:], 0.0), inc="init")
        for co in range(NCH):
            P.op("vector", lambda e, co=co: e.memset(
                g["ind"][0:64, co, 2 * co:2 * co + 1], 1.0), inc="init")
            P.op("vector", lambda e, co=co: e.memset(
                g["ind"][64:128, co, 2 * co + 1:2 * co + 2], 1.0), inc="init")
        P.op("vector", lambda e: e.memset(g["qnmax"][:], 0.0), inc="init")
        P.op("vector", lambda e: e.memset(g["knmax"][:], 0.0), inc="init")

        with ExitStack() as ab:
            vaug = ab.enter_context(
                nc.sbuf_tensor("vaug", [128, NT128, H, HD + 1], BF16))
            P.op("vector", lambda e: e.memset(vaug[:], 1.0), inc="init")
            with ExitStack() as a_st:
                emit_phase_A(nc, P, a_st, d, g, vaug)
            P.barrier()
            if dev_phase >= 2:
                with ExitStack() as b_st:
                    emit_phase_B(nc, P, b_st, d, g, vaug)
                P.barrier()

        if dev_phase >= 3:
            with ExitStack() as fs:
                emit_phase_F(nc, P, fs, d, g)
        elif dev_phase == 2:
            bufo = st.enter_context(nc.sbuf_tensor("bufo", [128, D], F32))
            for i in range(NT128):
                P.op("sync", lambda e, i=i: e.dma_start(
                    bufo[:], d["y1f"][i * 128:(i + 1) * 128, :]),
                    waits=["y1st0", "y1st1", ("outst", None)],
                    inc="devld", dma=True)
                P.op("sync", lambda e, i=i: e.dma_start(
                    d["out"][i * 128:(i + 1) * 128, :], bufo[:]),
                    waits=["devld"], inc="outst", dma=True)
        else:
            bufo = st.enter_context(nc.sbuf_tensor("bufo", [128, D], F32))
            P.op("vector", lambda e: e.memset(bufo[:], 0.0), inc="z")
            for i in range(NT128):
                P.op("sync", lambda e, i=i: e.dma_start(
                    d["out"][i * 128:(i + 1) * 128, :], bufo[:]),
                    waits=["z"], inc="outst", dma=True)

        P.barrier()
        with nc.Block() as block:
            P.emit(block)

    print(f"[build] ops per engine: {P.n_ops} "
          f"total={sum(P.n_ops.values())} sems={len(P.sems)}")
    return nc


ab_dbg = []


def emit_phase_A(nc, P, ab, d, g, vaug):
    sbA = lambda name, shape, dt=BF16: ab.enter_context(
        nc.sbuf_tensor(name, shape, dt))
    psA = lambda name, shape: ab.enter_context(
        nc.psum_tensor(name, shape, F32))

    w_wq = sbA("w_wq", [128, NCH, D])
    w_wk = sbA("w_wk", [128, NCH, D])
    w_wv = sbA("w_wv", [128, NCH, D])
    w_wd = sbA("w_wd", [128, NCH, DL])
    w_uk = sbA("w_uk", [128, 2, D])
    w_uv = sbA("w_uv", [128, 2, D])
    xbb = [sbA(f"xbb{p}", [128, 4, D]) for p in range(2)]
    xfm = [sbA(f"xfm{p}", [128, NCH, 512]) for p in range(2)]
    kv_sb = sbA("kv_sb", [128, NCH, 512])
    lat_sb = sbA("lat_sb", [128, 2, 512])
    q_sb = sbA("q_sb", [128, NCH, 512])
    k_sb = sbA("k_sb", [128, NCH, 512])
    sqq = [sbA(f"sqq{p}", [128, 512]) for p in range(2)]
    tmx = sbA("tmx", [16, 1], F32)
    c0one = sbA("c0one", [1, 1], F32)

    ab_dbg.clear()
    if "dbg" in d:
        ab_dbg.append(sbA("dbg_sb", [64, 512], F32))
    pQ = psA("pQ", [128, 512])
    pK = psA("pK", [128, 512])
    pV = psA("pV", [128, 512])
    pVT = psA("pVT", [128, 1024])
    pNQ = psA("pNQ", [16, 512])
    pNK = psA("pNK", [16, 512])
    pC0 = psA("pC0", [128, 1])

    for dst, src in [(w_wq, d["wq"]), (w_wk, d["wk"]), (w_wv, d["wv"]),
                     (w_wd, d["wdown"]), (w_uk, d["wupk"]), (w_uv, d["wupv"])]:
        P.op("gpsimd", lambda e, dst=dst, src=src: e.dma_start(
            dst[:], src.rearrange("(c p) o -> p c o", p=128)),
            inc="wA", dma=True)

    xfm_guard = {0: 0, 1: 0}
    for tb in range(8):
        t0 = tb * 512
        par = tb % 2
        P.op("gpsimd", lambda e, t0=t0, par=par: e.dma_start(
            xbb[par][:],
            d["x"][t0:t0 + 512, :].rearrange("(i p) f -> p i f", p=128)),
            waits=[(f"xst{par}", None)], inc=f"xcast{par}", dma=True)
        P.op("sync", lambda e, t0=t0, par=par: e.dma_start(
            d["xbf"][t0:t0 + 512, :].rearrange("(i p) f -> p i f", p=128),
            xbb[par][:]),
            waits=[f"xcast{par}"], inc=f"xst{par}", dma=True)
        P.op("scalar", lambda e, t0=t0, par=par: e.dma_start(
            xfm[par][:], d["xbf"][t0:t0 + 512, :], transpose=True),
            waits=[f"xst{par}", ("pe_v", xfm_guard[par])],
            inc=f"xfm{par}", dma=True)

        for co in range(NCH):
            def qchain(e, co=co, par=par):
                r = None
                for ci in range(NCH):
                    r = e.matmul(pQ[:], w_wq[:, ci, co * 128:(co + 1) * 128],
                                 xfm[par][:, ci, :],
                                 start=(ci == 0), stop=(ci == NCH - 1))
                return r
            P.op("tensor", qchain,
                 waits=["wA", f"xfm{par}", ("evq", None), ("latcp", None)],
                 inc="pe_q")
            P.op("vector", lambda e, co=co: e.tensor_scalar(
                q_sb[:, co, :], pQ[:], INV_SQRT_D, None, OP.mult),
                waits=[("pe_q", None), ("qst", None), ("sq_q", None)],
                inc="evq")
            sp = co % 2
            P.op("scalar", lambda e, co=co, sp=sp: e.activation(
                sqq[sp][:], q_sb[:, co, :], AF.Square),
                waits=[("evq", None), ("pe_nq", None), ("pe_nk", None)],
                inc="sq_q")
            P.op("tensor", lambda e, co=co, sp=sp: e.matmul(
                pNQ[:], g["ind"][:, co, :], sqq[sp][:],
                start=(co == 0), stop=(co == NCH - 1)),
                waits=[("sq_q", None), ("nqred", None), "init"], inc="pe_nq")

            def kchain(e, co=co, par=par):
                r = None
                for ci in range(NCH):
                    r = e.matmul(pK[:], w_wk[:, ci, co * 128:(co + 1) * 128],
                                 xfm[par][:, ci, :],
                                 start=(ci == 0), stop=(ci == NCH - 1))
                return r
            P.op("tensor", kchain, waits=[("kcp", None), ("krcp", None)],
                 inc="pe_k")
            P.op("scalar", lambda e, co=co: e.copy(kv_sb[:, co, :], pK[:]),
                 waits=[("pe_k", None), ("pe_lat", None)], inc="kcp")

            def vchain(e, co=co, par=par):
                r = None
                for ci in range(NCH):
                    r = e.matmul(pV[:], w_wv[:, ci, co * 128:(co + 1) * 128],
                                 xfm[par][:, ci, :],
                                 start=(ci == 0), stop=(ci == NCH - 1))
                return r
            P.op("tensor", vchain, waits=[("kvadd", None)], inc="pe_v")
            P.op("vector", lambda e, co=co: e.tensor_tensor(
                kv_sb[:, co, :], pV[:], kv_sb[:, co, :], OP.add),
                waits=[("pe_v", None), ("kcp", None)], inc="kvadd")
        xfm_guard[par] = P.count("pe_v")

        for lco in range(2):
            def latchain(e, lco=lco):
                r = None
                for ci in range(NCH):
                    r = e.matmul(pQ[:], w_wd[:, ci, lco * 128:(lco + 1) * 128],
                                 kv_sb[:, ci, :],
                                 start=(ci == 0), stop=(ci == NCH - 1))
                return r
            P.op("tensor", latchain,
                 waits=[("kvadd", None), ("latcp", None), ("evq", None)],
                 inc="pe_lat")
            P.op("scalar", lambda e, lco=lco: e.copy(lat_sb[:, lco, :], pQ[:]),
                 waits=[("pe_lat", None), ("pe_krec", None), ("pe_vt", None)],
                 inc="latcp")

        for co in range(NCH):
            def krchain(e, co=co):
                r = None
                for ci in range(2):
                    r = e.matmul(pK[:], w_uk[:, ci, co * 128:(co + 1) * 128],
                                 lat_sb[:, ci, :],
                                 start=(ci == 0), stop=(ci == 1))
                return r
            P.op("tensor", krchain,
                 waits=[("latcp", None), ("krcp", None), ("kcp", None)],
                 inc="pe_krec")
            P.op("scalar", lambda e, co=co: e.copy(k_sb[:, co, :], pK[:]),
                 waits=[("pe_krec", None), ("kst", None), ("sq_k", None)],
                 inc="krcp")
            sp = co % 2
            P.op("scalar", lambda e, co=co, sp=sp: e.activation(
                sqq[sp][:], k_sb[:, co, :], AF.Square),
                waits=[("krcp", None), ("pe_nk", None), ("pe_nq", None)],
                inc="sq_k")
            P.op("tensor", lambda e, co=co, sp=sp: e.matmul(
                pNK[:], g["ind"][:, co, :], sqq[sp][:],
                start=(co == 0), stop=(co == NCH - 1)),
                waits=[("sq_k", None), ("nkred", None)], inc="pe_nk")

        for i in range(4):
            def vtchain(e, i=i):
                r = None
                for n in range(2):
                    for ci in range(2):
                        r = e.matmul(
                            pVT[:, n * 512:(n + 1) * 512],
                            lat_sb[:, ci, i * 128:(i + 1) * 128],
                            w_uv[:, ci, n * 512:(n + 1) * 512],
                            start=(ci == 0), stop=(ci == 1))
                return r
            P.op("tensor", vtchain, waits=[("latcp", None), ("vev", None)],
                 inc="pe_vt")
            gi = tb * 4 + i
            P.op("vector", lambda e, gi=gi: e.tensor_copy(
                vaug[:, gi, :, 0:HD],
                pVT.ap().rearrange("p (h v) -> p h v", v=64)),
                waits=[("pe_vt", None), "init"], inc="vev")

        P.op("vector", lambda e: e.reduce_max(tmx[:], pNQ[:], axis=AX.X),
             waits=[("pe_nq", None), ("nqmax", None)], inc="nqred")
        P.op("vector", lambda e: e.tensor_tensor(
            g["qnmax"][:], tmx[:], g["qnmax"][:], OP.max),
            waits=["init", ("nqred", None), ("nqmax", None)], inc="nqmax")
        P.op("vector", lambda e: e.reduce_max(tmx[:], pNK[:], axis=AX.X),
             waits=[("pe_nk", None), ("nqmax", None), ("nkmax", None)],
             inc="nkred")
        P.op("vector", lambda e: e.tensor_tensor(
            g["knmax"][:], tmx[:], g["knmax"][:], OP.max),
            waits=[("nkred", None), ("nkmax", None)], inc="nkmax")

        P.op("sync", lambda e, t0=t0: e.dma_start(
            d["qf"].rearrange("(c p) t -> p c t", p=128)[:, :, t0:t0 + 512],
            q_sb[:]), waits=[("evq", None)], inc="qst", dma=True)
        P.op("sync", lambda e, t0=t0: e.dma_start(
            d["kf"].rearrange("(c p) t -> p c t", p=128)[:, :, t0:t0 + 512],
            k_sb[:]), waits=[("krcp", None)], inc="kst", dma=True)
        if "nrm" in d:
            P.op("sync", lambda e, tb=tb: e.dma_start(
                d["nrm"][tb, 0, :], g["qnmax"][:]),
                waits=[("nqmax", None)], inc="nrmst", dma=True)
            P.op("sync", lambda e, tb=tb: e.dma_start(
                d["nrm"][tb, 1, :], g["knmax"][:]),
                waits=[("nkmax", None)], inc="nrmst", dma=True)
        if "dbg" in d and tb == 0:
            dbg = ab_dbg[0]
            P.op("vector", lambda e: e.tensor_copy(dbg[0:16, :], pNQ[:]),
                 waits=[("pe_nq", None)], inc="dbgv")
            P.op("vector", lambda e: e.tensor_copy(dbg[32:48, :], pNK[:]),
                 waits=[("pe_nk", None)], inc="dbgv")
            P.op("sync", lambda e: e.dma_start(d["dbg"][0:16, :],
                                               dbg[0:16, :]),
                 waits=["dbgv"], inc="dbgst", dma=True)
            P.op("sync", lambda e: e.dma_start(d["dbg"][16:32, :],
                                               dbg[32:48, :]),
                 waits=["dbgv"], inc="dbgst", dma=True)

    P.op("sync", lambda e: e.dma_start(d["c0b"][0:16], g["qnmax"][:]),
         waits=[("nqmax", None)], inc="c0st", dma=True)
    P.op("sync", lambda e: e.dma_start(d["c0b"][16:32], g["knmax"][:]),
         waits=[("nkmax", None)], inc="c0st", dma=True)
    P.op("sync", lambda e: e.dma_start(
        g["r32"][:], d["c0b"].rearrange("(o a) -> o a", o=1)),
        waits=["c0st"], inc="c0ld", dma=True)
    P.op("vector", lambda e: e.reduce_max(
        g["t11"][:, 0:1], g["r32"][:, 0:16], axis=AX.X),
        waits=["c0ld"], inc="c0v")
    P.op("vector", lambda e: e.reduce_max(
        g["t11"][:, 1:2], g["r32"][:, 16:32], axis=AX.X), inc="c0v")
    P.op("vector", lambda e: e.tensor_tensor(
        g["t11"][:, 2:3], g["t11"][:, 0:1], g["t11"][:, 1:2], OP.mult),
        waits=[("c0v", None)], inc="c0v")
    P.op("scalar", lambda e: e.activation(
        g["t11"][:, 3:4], g["t11"][:, 2:3], AF.Ln),
        waits=[("c0v", None)], inc="c0a")
    P.op("scalar", lambda e: e.activation(
        c0one[:], g["t11"][:, 3:4], AF.Exp, scale=0.5),
        waits=[("c0a", None)], inc="c0a")
    P.op("vector", lambda e: e.tensor_scalar(
        c0one[:], c0one[:], -1.0, None, OP.mult),
        waits=[("c0a", None)], inc="c0n")
    P.op("tensor", lambda e: e.matmul(
        pC0[:], g["ones"][:, 0:128], c0one[:], start=True, stop=True),
        waits=[("c0n", None)], inc="c0mm")
    P.op("scalar", lambda e: e.copy(g["c0neg"][:], pC0[:]),
         waits=[("c0mm", None)], inc="c0done")


def emit_phase_B(nc, P, ab, d, g, vaug):
    sbB = lambda name, shape, dt=BF16: ab.enter_context(
        nc.sbuf_tensor(name, shape, dt))
    psB = lambda name, shape: ab.enter_context(
        nc.psum_tensor(name, shape, F32))

    w_wo = sbB("w_wo", [128, NCH, D])
    Ksb = sbB("Ksb", [128, NCH, S])
    Qsb = sbB("Qsb", [128, NCH, 512])
    dec = sbB("dec", [128, 16, 512], F32)
    esb = [sbB(f"esb{p}", [128, 512], F32) for p in range(4)]
    PT8 = sbB("PT8", [128, 8, 512])
    _l = sbB("l_sb", [1, 512], F32)
    _r = sbB("r_sb", [1, 512], F32)
    l_sb = [_l, _l]
    r_sb = [_r, _r]
    _rb = sbB("rb_sb", [64, 512], F32)
    rb_sb = [_rb, _rb]
    o_all = sbB("o_all", [128, NCH, 512])
    xres = [sbB(f"xres{p}", [128, D], F32) for p in range(2)]
    _xa = sbB("xa", [128, D], F32)
    xa = [_xa, _xa]
    _y1fb = sbB("y1fb", [128, D], F32)
    y1fb = [_y1fb, _y1fb]
    y1bb = [sbB(f"y1bb{p}", [128, D]) for p in range(2)]
    stat = [sbB(f"stat{p}", [128, 8], F32) for p in range(2)]

    pSq = [psB(f"pSq{p}", [128, 512]) for p in range(4)]
    pO = [psB(f"pO{p}", [HD + 1, 512]) for p in range(2)]
    pAO = psB("pAO", [128, 1024])

    P.op("gpsimd", lambda e: e.dma_start(
        w_wo[:], d["wo"].rearrange("(c p) o -> p c o", p=128)),
        inc="wold", dma=True)

    gd = {"tt": [0, 0], "ex": [0, 0], "pv_pt": [0, 0],
          "lcp": [0, 0], "oev": [0, 0], "rbc": [0, 0]}

    for b in range(B_PER_CORE):
        P.op("sync", lambda e, b=b: e.dma_start(
            Ksb[:],
            d["kf"][:, b * S:(b + 1) * S].rearrange("(c p) t -> p c t",
                                                    p=128)),
            waits=[("pe_sc", None)], inc="Kld", dma=True)
        for qh in range(4):
            q0 = b * S + qh * 512
            P.op("sync", lambda e, q0=q0: e.dma_start(
                Qsb[:],
                d["qf"][:, q0:q0 + 512].rearrange("(c p) t -> p c t", p=128)),
                waits=[("pe_sc", None)], inc="Qld", dma=True)
            P.op("sync", lambda e, qh=qh: e.dma_start(
                dec[:],
                d["decay"][:, qh * 512:(qh + 1) * 512]
                .rearrange("(c p) q -> p c q", p=128)),
                waits=[("tt", None)], inc="decld", dma=True)
            for i in range(2):
                pr = i % 2
                P.op("sync", lambda e, q0=q0, i=i, pr=pr: e.dma_start(
                    xres[pr][:], d["x"][q0 + i * 128:q0 + (i + 1) * 128, :]),
                    waits=[(f"lnadd{pr}", None)], inc=f"xres{pr}", dma=True)

            cnt = {"tt": {}, "ex": {}, "pv": {}}

            def emit_pv(s, hh):
                hp, c = s // 16, s % 16
                h = 2 * hp + hh
                po = hh
                gi = b * 16
                w = [("ex", cnt["ex"][(s, hh)])]
                if c == 0:
                    w += [("lcp", gd["lcp"][po]), ("oev", gd["oev"][po])]
                sl = (s % 4) * 2 + hh
                P.op("tensor", lambda e, gi=gi, h=h, c=c, po=po, sl=sl:
                     e.matmul(pO[po][:], vaug[:, gi + c, h, :],
                              PT8[:, sl, :], start=(c == 0), stop=(c == 15)),
                     waits=w, inc="pe_pv")
                cnt["pv"][(s, hh)] = P.count("pe_pv")
                if c == 15:
                    emit_evac(h, po)

            def emit_evac(h, po):
                ch2, hb2 = h // 2, (h % 2) * 64
                P.op("scalar", lambda e, po=po: e.copy(
                    l_sb[po][:], pO[po][64:65, :]),
                    waits=[("pe_pv", None)], inc="lcp")
                P.op("scalar", lambda e, po=po: e.activation(
                    r_sb[po][:], l_sb[po][:], AF.Ln),
                    waits=[("lcp", None), ("pe_rb", None)], inc="lcp")
                P.op("scalar", lambda e, po=po: e.activation(
                    r_sb[po][:], r_sb[po][:], AF.Exp, scale=-1.0),
                    waits=[("lcp", None)], inc="lcp")
                gd["lcp"][po] = P.count("lcp")
                P.op("tensor", lambda e, po=po: e.matmul(
                    pAO[0:64, 0:512], g["ones"][:, 0:64], r_sb[po][:],
                    start=True, stop=True),
                    waits=[("lcp", None), ("rbc", None),
                           ("lnadd0", None), ("lnadd1", None)], inc="pe_rb")
                P.op("scalar", lambda e, po=po: e.copy(
                    rb_sb[po][:], pAO[0:64, 0:512]),
                    waits=[("pe_rb", None), ("oev", None)], inc="rbc")
                gd["rbc"][po] = P.count("rbc")
                P.op("vector",
                     lambda e, ch2=ch2, hb2=hb2, po=po: e.tensor_tensor(
                         o_all[hb2:hb2 + 64, ch2, :], pO[po][0:64, :],
                         rb_sb[po][:], OP.mult),
                     waits=[("rbc", None), ("pe_wo", None)], inc="oev")
                gd["oev"][po] = P.count("oev")

            for s in range(128):
                hp, c = s // 16, s % 16
                cp = c % 2

                def smm2(e, hp=hp, c=c, cp=cp):
                    e.matmul(pSq[cp][:], Ksb[0:64, hp, c * 128:c * 128 + 128],
                             Qsb[0:64, hp, :], start=True, stop=True)
                    return e.matmul(pSq[2 + cp][:],
                                    Ksb[64:128, hp, c * 128:c * 128 + 128],
                                    Qsb[64:128, hp, :],
                                    start=True, stop=True)
                P.op("tensor", smm2,
                     waits=["Kld", "Qld", "c0done",
                            ("tt", cnt["tt"].get((s - 2, 1), 0))],
                     inc="pe_sc")
                for hh in range(2):
                    P.op("vector", lambda e, c=c, cp=cp, hh=hh:
                         e.tensor_tensor(esb[cp * 2 + hh][:],
                                         pSq[hh * 2 + cp][:],
                                         dec[:, c, :], OP.mult),
                         waits=[("pe_sc", None), "decld",
                                ("ex", cnt["ex"].get((s - 2, hh), 0))],
                         inc="tt")
                    cnt["tt"][(s, hh)] = P.count("tt")
                    sl = (s % 4) * 2 + hh
                    P.op("scalar", lambda e, cp=cp, hh=hh, sl=sl:
                         e.activation(PT8[:, sl, :], esb[cp * 2 + hh][:],
                                      AF.Exp, bias=g["c0neg"][:], scale=1.0),
                         waits=[("tt", None),
                                ("pe_pv", cnt["pv"].get((s - 4, hh), 0))],
                         inc="ex")
                    cnt["ex"][(s, hh)] = P.count("ex")
                if s >= 2:
                    emit_pv(s - 2, 0)
                    emit_pv(s - 2, 1)
            for s in (126, 127):
                emit_pv(s, 0)
                emit_pv(s, 1)

            for i in range(4):
                pr = i % 2
                tg = q0 + i * 128

                def wochain(e, i=i):
                    r = None
                    for n in range(2):
                        for ci in range(NCH):
                            r = e.matmul(
                                pAO[:, n * 512:(n + 1) * 512],
                                o_all[:, ci, i * 128:(i + 1) * 128],
                                w_wo[:, ci, n * 512:(n + 1) * 512],
                                start=(ci == 0), stop=(ci == NCH - 1))
                    return r
                P.op("tensor", wochain,
                     waits=[("oev", None), "wold", ("rbc", None),
                            ("lnadd0", None), ("lnadd1", None)], inc="pe_wo")
                P.op("vector", lambda e, pr=pr: e.tensor_tensor(
                    xa[pr][:], pAO[:], xres[pr][:], OP.add),
                    waits=[("pe_wo", None), (f"xres{pr}", None),
                           ("y1w0", None), ("y1w1", None), ("lnsq", None),
                           ("rstd0", None), ("rstd1", None)],
                    inc=f"lnadd{pr}")
                P.op("vector", lambda e, pr=pr: e.reduce_sum(
                    stat[pr][:, 0:1], xa[pr][:], axis=AX.X),
                    waits=[(f"lnadd{pr}", None)], inc=f"lnst{pr}")
                P.op("scalar", lambda e, pr=pr: e.activation(
                    y1bb[pr][:], xa[pr][:], AF.Square,
                    accum_out=stat[pr][:, 1:2]),
                    waits=[(f"lnadd{pr}", None), ("lnsq", None),
                           (f"y1st{pr}", None)], inc="lnsq")

                def lnchain(e, pr=pr):
                    ch = g["chV"]
                    cstep(e, ch, e.tensor_scalar(
                        stat[pr][:, 2:3], stat[pr][:, 0:1], 1.0 / D, None,
                        OP.mult))
                    cstep(e, ch, e.tensor_scalar(
                        stat[pr][:, 3:4], stat[pr][:, 1:2], 1.0 / D, None,
                        OP.mult))
                    cstep(e, ch, e.tensor_mul(
                        stat[pr][:, 4:5], stat[pr][:, 2:3],
                        stat[pr][:, 2:3]))
                    cstep(e, ch, e.tensor_sub(
                        stat[pr][:, 5:6], stat[pr][:, 3:4],
                        stat[pr][:, 4:5]))
                    return e.tensor_scalar(stat[pr][:, 5:6],
                                           stat[pr][:, 5:6], EPS, None,
                                           OP.add)
                P.op("vector", lnchain,
                     waits=[("lnsq", None), (f"lnst{pr}", None)],
                     inc=f"lnst{pr}")

                def rstdchain(e, pr=pr):
                    cstep(e, g["chS"], e.activation(
                        stat[pr][:, 6:7], stat[pr][:, 5:6], AF.Ln))
                    return e.activation(stat[pr][:, 7:8], stat[pr][:, 6:7],
                                        AF.Exp, scale=-0.5)
                P.op("scalar", rstdchain, waits=[(f"lnst{pr}", None)],
                     inc=f"rstd{pr}")
                P.op("vector", lambda e, pr=pr: e.tensor_scalar(
                    y1fb[pr][:], xa[pr][:], stat[pr][:, 2:3],
                    stat[pr][:, 7:8], OP.subtract, OP.mult),
                    waits=[(f"rstd{pr}", None), ("y1st0", None),
                           ("y1st1", None)],
                    inc=f"y1w{pr}")
                P.op("vector", lambda e, pr=pr: e.tensor_scalar(
                    y1bb[pr][:], xa[pr][:], stat[pr][:, 2:3],
                    stat[pr][:, 7:8], OP.subtract, OP.mult),
                    inc=f"y1w{pr}")
                P.op("sync", lambda e, tg=tg, pr=pr: e.dma_start(
                    d["y1f"][tg:tg + 128, :], y1fb[pr][:]),
                    waits=[(f"y1w{pr}", None)], inc=f"y1st{pr}", dma=True)
                P.op("sync", lambda e, tg=tg, pr=pr: e.dma_start(
                    d["y1b"][tg:tg + 128, :], y1bb[pr][:]),
                    inc=f"y1st{pr}", dma=True)
                if i + 2 < 4:
                    P.op("sync", lambda e, q0=q0, i=i, pr=pr: e.dma_start(
                        xres[pr][:],
                        d["x"][q0 + (i + 2) * 128:q0 + (i + 3) * 128, :]),
                        waits=[(f"lnadd{pr}", None)], inc=f"xres{pr}",
                        dma=True)


def emit_phase_F(nc, P, fs, d, g):
    sbF = lambda name, shape, dt=BF16: fs.enter_context(
        nc.sbuf_tensor(name, shape, dt))
    psF = lambda name, shape: fs.enter_context(
        nc.psum_tensor(name, shape, F32))

    ring = [sbF(f"ring{s}", [128, NCH, 512]) for s in range(4)]
    dpb = [sbF(f"dpb{p}", [128, NI, D]) for p in range(2)]
    sg_sb = sbF("sg_sb", [128, NI, TSB])
    act_sb = sbF("act_sb", [128, NI, TSB])
    y1fm = [sbF(f"y1fm{p}", [128, NCH, TSB]) for p in range(2)]
    yacc = sbF("yacc", [128, TSB // 128, D], F32)
    gw_sb = sbF("gw_sb", [128, NCH, E])
    gb_sb = sbF("gb_sb", [1, E], F32)
    tn_sb = sbF("tn_sb", [1, E], F32)
    gbb = sbF("gbb", [128, E], F32)
    tnb = sbF("tnb", [128, E], F32)
    scr = [sbF(f"scr{p}", [128, 128], F32) for p in range(2)]
    cw = [sbF(f"cw{p}", [128, TSB // 128, E], F32) for p in range(2)]
    gwk = sbF("gwk", [128, 96], F32)
    y1tm = [sbF(f"y1tm{p}", [128, D], F32) for p in range(2)]
    outb = [sbF(f"outb{p}", [128, D], F32) for p in range(2)]
    sqd = sbF("sqdF", [128, D])
    stA = sbF("stA", [128, 8], F32)
    stB = sbF("stB", [128, 8], F32)
    stC = sbF("stC", [128, 8], F32)

    pG = [psF(f"pG{p}", [128, 512]) for p in range(2)]
    pU = [psF(f"pU{p}", [128, 512]) for p in range(2)]
    pY = [psF(f"pY{p}", [128, 1024]) for p in range(2)]

    NT = TSB // 128

    P.op("gpsimd", lambda e: e.dma_start(
        gw_sb[:], d["gwT"].rearrange("(c p) o -> p c o", p=128)),
        inc="gwld", dma=True)
    P.op("sync", lambda e: e.dma_start(gb_sb[:], d["gbias"][:, :]),
         inc="gbld", dma=True)
    P.op("sync", lambda e: e.dma_start(tn_sb[:], d["tinyc"][:, :]),
         inc="gbld", dma=True)
    P.op("tensor", lambda e: e.matmul(pG[0][:, 0:16], g["ones"][:, 0:128],
                                      gb_sb[:], start=True, stop=True),
         waits=["gbld"], inc="pe_gc")
    P.op("tensor", lambda e: e.matmul(pG[1][:, 0:16], g["ones"][:, 0:128],
                                      tn_sb[:], start=True, stop=True),
         inc="pe_gc")
    P.op("scalar", lambda e: e.copy(gbb[:], pG[0][:, 0:16]),
         waits=[("pe_gc", None)], inc="gc")
    P.op("scalar", lambda e: e.copy(tnb[:], pG[1][:, 0:16]), inc="gc")

    slot_state = [0] * 4
    next_slot = [0]
    dp_guard = {0: 0, 1: 0}
    uact_names = ("uact0", "uact1")
    pairs = [(i, j) for i in range(4) for j in range(i + 1, 4)]

    def load_ring(src_ap):
        s = next_slot[0] % 4
        next_slot[0] += 1
        guard = slot_state[s]
        w = src_ap.shape[-1]
        P.op("gpsimd", lambda e, s=s, w=w, src_ap=src_ap: e.dma_start(
            ring[s][:, :, 0:w], src_ap),
            waits=[("pe_gu", guard)], inc=f"ring{s}", dma=True)
        return s

    gate_state = [None]

    def gchain_emit(i8, par):
        def gchain(e, i8=i8, par=par):
            ch = g["chV"]
            cs = lambda inst: cstep(e, ch, inst)
            sc = scr[par][:, i8 * 16:(i8 + 1) * 16]
            sfc = gwk[:, 0:16]
            cs(e.tensor_tensor(sfc, sc, gbb[:], OP.add))
            sfg = sfc.rearrange("p (gg m) -> p gg m", m=4)
            for k, (i, j) in enumerate(pairs):
                cs(e.tensor_tensor(gwk[:, 24 + k * 4:28 + k * 4],
                                   sfg[:, :, i], sfg[:, :, j], OP.add))
            gs = gwk[:, 16:20]
            cs(e.tensor_tensor(gs, gwk[:, 24:28], gwk[:, 28:32], OP.max))
            for k in range(2, 6):
                cs(e.tensor_tensor(gs, gs, gwk[:, 24 + k * 4:28 + k * 4],
                                   OP.max))
            for k, (i, j) in enumerate(pairs):
                cs(e.tensor_tensor(gwk[:, 48 + k:49 + k], gs[:, i:i + 1],
                                   gs[:, j:j + 1], OP.min))
            cs(e.tensor_tensor(gwk[:, 54:55], gwk[:, 48:49], gwk[:, 49:50],
                               OP.max))
            for k in range(2, 6):
                cs(e.tensor_tensor(gwk[:, 54:55], gwk[:, 54:55],
                                   gwk[:, 48 + k:49 + k], OP.max))
            gmask = gwk[:, 20:24]
            cs(e.tensor_scalar(gmask, gs, gwk[:, 54:55], None, OP.is_ge))
            gmb = bass.AP(gmask.tensor, gmask.offset,
                          [gmask.ap[0], [1, 4], [0, 4]])
            tmp = gwk[:, 32:48]
            cs(e.tensor_tensor(tmp, sfc, tnb[:], OP.subtract))
            cs(e.tensor_tensor(tmp, tmp, gmb, OP.mult))
            cs(e.tensor_tensor(tmp, tmp, tnb[:], OP.add))
            w = gwk[:, 0:16]
            cs(e.tensor_copy(w, tmp))
            msk = gwk[:, 64:80]
            mx = gwk[:, 80:81]
            for _ in range(3):
                cs(e.reduce_max(mx, w, axis=AX.X))
                cs(e.tensor_scalar(msk, w, mx, None, OP.is_ge))
                cs(e.scalar_tensor_tensor(w, msk, -1e38, w, OP.mult,
                                          OP.add))
            cs(e.reduce_max(mx, w, axis=AX.X))
            sel = gwk[:, 64:80]
            cs(e.tensor_scalar(sel, tmp, mx, None, OP.is_ge))
            wu = gwk[:, 32:48]
            cs(e.tensor_tensor(wu, sel, sc, OP.mult))
            cs(e.reduce_sum(gwk[:, 81:82], wu, axis=AX.X))
            cs(e.tensor_scalar(gwk[:, 81:82], gwk[:, 81:82], 1e-20, None,
                               OP.add))
            cs(e.reciprocal(gwk[:, 82:83], gwk[:, 81:82]))
            return e.tensor_scalar(cw[par][:, i8, :], wu, gwk[:, 82:83],
                                   None, OP.mult)
        P.op("vector", gchain, waits=[("glog", None), "gc"], inc="gdve")

    def gate_emit(tsb):
        par = tsb % 2
        t0 = tsb * TSB
        P.op("scalar", lambda e, t0=t0, par=par: e.dma_start(
            y1fm[par][:], d["y1b"][t0:t0 + TSB, :], transpose=True),
            waits=[("pe_gu", None), ("pe_glog", None), "y1st0", "y1st1"],
            inc=f"y1fm{par}", dma=True)

        def logits(e, par=par):
            r = None
            for i8 in range(NT):
                for ci in range(NCH):
                    r = e.matmul(pU[0][:, i8 * 16:(i8 + 1) * 16],
                                 y1fm[par][:, ci, i8 * 128:(i8 + 1) * 128],
                                 gw_sb[:, ci, :],
                                 start=(i8 == 0 and ci == 0),
                                 stop=(i8 == NT - 1 and ci == NCH - 1),
                                 skip_group_check=not (
                                     (i8 == 0 and ci == 0)
                                     or (i8 == NT - 1 and ci == NCH - 1)))
            return r
        P.op("tensor", logits,
             waits=[f"y1fm{par}", "gwld", "gc", ("glog", None),
                    ("uact0", None), ("uact1", None)], inc="pe_glog")
        P.op("scalar", lambda e, par=par: e.activation(
            scr[par][:], pU[0][:, 0:128], AF.Sigmoid),
            waits=[("pe_glog", None), ("gdve", None)], inc="glog")

        gate_state[0] = (tsb, par)
        if "cwd" in d:
            P.op("sync", lambda e, tsb=tsb, par=par: e.dma_start(
                d["cwd"][tsb], cw[par][:]), waits=["gdve"], inc="dvst",
                dma=True)
            P.op("sync", lambda e, tsb=tsb, par=par: e.dma_start(
                d["scrd"][tsb], scr[par][:]), waits=["glog"], inc="dvst",
                dma=True)

    def expert_emit(tsb, ee):
        par = tsb % 2
        routed = ee < E
        se = ee - E
        if routed:
            gsrc, usrc, dsrc = d["gpT"][ee], d["upT"][ee], d["dpT"][ee]
        else:
            gsrc = d["sgT"][:, se * I:(se + 1) * I]
            usrc = d["suT"][:, se * I:(se + 1) * I]
            dsrc = d["sdT"][se * I:(se + 1) * I, :]

        widths = [512, 512, 384]
        gslots = [load_ring(gsrc[:, j * 512:j * 512 + w]
                            .rearrange("(c p) i -> p c i", p=128))
                  for j, w in enumerate(widths)]
        dpar = ee % 2
        P.op("gpsimd", lambda e, dsrc=dsrc, dpar=dpar: e.dma_start(
            dpb[dpar][:], dsrc.rearrange("(c p) h -> p c h", p=128)),
            waits=[("pe_dn", dp_guard[dpar])], inc=f"dpld{dpar}",
            dma=True)

        for ic in range(NI):
            sl = gslots[ic // 4]
            off = (ic % 4) * 128
            for th in range(2):
                gp = th

                def gmm(e, ic=ic, th=th, sl=sl, off=off, gp=gp, par=par):
                    r = None
                    for ci in range(NCH):
                        r = e.matmul(
                            pG[gp][:],
                            ring[sl][:, ci, off:off + 128],
                            y1fm[par][:, ci, th * 512:(th + 1) * 512],
                            start=(ci == 0), stop=(ci == NCH - 1))
                    return r
                P.op("tensor", gmm,
                     waits=[(f"ring{sl}", None), f"y1fm{par}", "gc",
                            (f"silu{gp}", None)],
                     inc="pe_gu")
                P.op("scalar",
                     lambda e, ic=ic, th=th, gp=gp: e.activation(
                         sg_sb[:, ic, th * 512:(th + 1) * 512],
                         pG[gp][:], AF.Silu),
                     waits=[("pe_gu", None), ("uact0", None),
                            ("uact1", None)],
                     inc=f"silu{gp}")
            slot_state[sl] = P.count("pe_gu")
        uslots = [load_ring(usrc[:, j * 512:j * 512 + w]
                            .rearrange("(c p) i -> p c i", p=128))
                  for j, w in enumerate(widths)]
        for ic in range(NI):
            sl = uslots[ic // 4]
            off = (ic % 4) * 128
            for th in range(2):
                up = th

                def umm(e, ic=ic, th=th, sl=sl, off=off, up=up, par=par):
                    r = None
                    for ci in range(NCH):
                        r = e.matmul(
                            pU[up][:],
                            ring[sl][:, ci, off:off + 128],
                            y1fm[par][:, ci, th * 512:(th + 1) * 512],
                            start=(ci == 0), stop=(ci == NCH - 1))
                    return r
                P.op("tensor", umm,
                     waits=[(f"ring{sl}", None), (uact_names[up], None),
                            ("pe_glog", None), ("glog", None)],
                     inc="pe_gu")
                P.op("vector",
                     lambda e, ic=ic, th=th, up=up: e.tensor_tensor(
                         act_sb[:, ic, th * 512:(th + 1) * 512],
                         pU[up][:],
                         sg_sb[:, ic, th * 512:(th + 1) * 512],
                         OP.mult),
                     waits=[("pe_gu", None), (f"silu{up}", None),
                            ("pe_dn", None)], inc=uact_names[up])
            slot_state[sl] = P.count("pe_gu")

        if "actd" in d and tsb == 0 and ee == 0:
            P.op("sync", lambda e: e.dma_start(d["actd"][:], act_sb[:]),
                 waits=["uact0", "uact1"], inc="dvst", dma=True)
            P.op("sync", lambda e: e.dma_start(d["sgd"][:], sg_sb[:]),
                 waits=["silu0", "silu1"], inc="dvst", dma=True)
        for i8 in range(NT):
            yp = i8 % 2

            def dmm(e, i8=i8, yp=yp, dpar=dpar):
                r = None
                for ic in range(NI):
                    for n in range(2):
                        r = e.matmul(
                            pY[yp][:, n * 512:(n + 1) * 512],
                            act_sb[:, ic, i8 * 128:(i8 + 1) * 128],
                            dpb[dpar][:, ic, n * 512:(n + 1) * 512],
                            start=(ic == 0), stop=(ic == NI - 1))
                return r
            P.op("tensor", dmm,
                 waits=[("uact0", None), ("uact1", None),
                        (f"dpld{dpar}", None), (f"yadd{yp}", None)],
                 inc="pe_dn")
            if routed and ee == 0:
                P.op("vector", lambda e, i8=i8, yp=yp, par=par:
                     e.tensor_scalar(
                         yacc[:, i8, :], pY[yp][:], cw[par][:, i8, 0:1],
                         None, OP.mult),
                     waits=[("pe_dn", None), ("gdve", None),
                            ("outw", None)], inc=f"yadd{yp}")
            elif routed:
                P.op("vector",
                     lambda e, i8=i8, yp=yp, ee=ee, par=par:
                     e.scalar_tensor_tensor(
                         yacc[:, i8, :], pY[yp][:],
                         cw[par][:, i8, ee:ee + 1], yacc[:, i8, :],
                         OP.mult, OP.add),
                     waits=[("pe_dn", None)], inc=f"yadd{yp}")
            else:
                P.op("vector", lambda e, i8=i8, yp=yp: e.tensor_tensor(
                    yacc[:, i8, :], pY[yp][:], yacc[:, i8, :], OP.add),
                    waits=[("pe_dn", None)], inc=f"yadd{yp}")
        dp_guard[dpar] = P.count("pe_dn")

    def finalize_emit(tsb):
        t0 = tsb * TSB
        if "yad" in d:
            P.op("sync", lambda e, tsb=tsb: e.dma_start(
                d["yad"][tsb], yacc[:]),
                waits=["yadd0", "yadd1"], inc="dvst", dma=True)
        for i8 in range(NT):
            pr = i8 % 2
            tg = t0 + i8 * 128
            P.op("sync", lambda e, tg=tg, pr=pr: e.dma_start(
                y1tm[pr][:], d["y1f"][tg:tg + 128, :]),
                waits=[(f"fln{pr}", None)], inc=f"y1tm{pr}", dma=True)
            P.op("vector", lambda e, i8=i8, pr=pr: e.tensor_tensor(
                yacc[:, i8, :], y1tm[pr][:], yacc[:, i8, :], OP.add),
                waits=[(f"y1tm{pr}", None), ("yadd0", None),
                       ("yadd1", None)], inc=f"fln{pr}")
            P.op("vector", lambda e, i8=i8: e.reduce_sum(
                stA[:, i8:i8 + 1], yacc[:, i8, :], axis=AX.X),
                waits=[(f"fln{pr}", None)], inc="flnr")
            P.op("scalar", lambda e, i8=i8: e.activation(
                sqd[:], yacc[:, i8, :], AF.Square,
                accum_out=stB[:, i8:i8 + 1]),
                waits=[(f"fln{pr}", None), ("flnsq", None)], inc="flnsq")
        def stats(e):
            ch = g["chV"]
            cstep(e, ch, e.tensor_scalar(stA[:], stA[:], 1.0 / D, None,
                                         OP.mult))
            cstep(e, ch, e.tensor_scalar(stB[:], stB[:], 1.0 / D, None,
                                         OP.mult))
            cstep(e, ch, e.tensor_mul(stC[:], stA[:], stA[:]))
            cstep(e, ch, e.tensor_sub(stB[:], stB[:], stC[:]))
            return e.tensor_scalar(stB[:], stB[:], EPS, None, OP.add)
        P.op("vector", stats, waits=[("flnr", None), ("flnsq", None)],
             inc="flns")

        def rstd8(e):
            cstep(e, g["chS"], e.activation(stC[:], stB[:], AF.Ln))
            return e.activation(stB[:], stC[:], AF.Exp, scale=-0.5)
        P.op("scalar", rstd8, waits=[("flns", None)], inc="flna")
        for i8 in range(NT):
            pr = i8 % 2
            tg = t0 + i8 * 128
            P.op("vector", lambda e, i8=i8, pr=pr: e.tensor_scalar(
                outb[pr][:], yacc[:, i8, :], stA[:, i8:i8 + 1],
                stB[:, i8:i8 + 1], OP.subtract, OP.mult),
                waits=[("flna", None), (f"outst{pr}", None)],
                inc="outw")
            P.op("sync", lambda e, tg=tg, pr=pr: e.dma_start(
                d["out"][tg:tg + 128, :], outb[pr][:]),
                waits=[("outw", None)], inc=f"outst{pr}", dma=True)

    for tsb in range(NTSB):
        if tsb == 0:
            gate_emit(0)
            for i8 in range(NT):
                gchain_emit(i8, 0)
            gate_state[0] = None
        for ee in range(DEV_NEXPERTS):
            expert_emit(tsb, ee)
            if ee == 7 and tsb + 1 < NTSB and DEV_NEXPERTS > 9:
                gate_emit(tsb + 1)
            if ee >= 9 and gate_state[0] is not None and ee <= 16:
                gtsb, gpar = gate_state[0]
                gchain_emit(ee - 9, gpar)
                if ee == 16:
                    gate_state[0] = None
        if DEV_NEXPERTS <= 9 and tsb + 1 < NTSB:
            gate_emit(tsb + 1)
            for i8 in range(NT):
                gchain_emit(i8, (tsb + 1) % 2)
            gate_state[0] = None
        finalize_emit(tsb)


_prog_cache = {}


def _get_prog(dev_phase=3):
    key = dev_phase
    if key not in _prog_cache:
        _prog_cache[key] = build_program(dev_phase=dev_phase)
    return _prog_cache[key]


def _stage_weights(inputs):
    asc = np.ascontiguousarray
    f = np.float32

    def a(x):
        return asc(np.asarray(x, dtype=f))
    w = {
        "wq": a(inputs["L0_attn_W_q"]),
        "wk": a(inputs["L0_attn_W_k"]),
        "wv": a(inputs["L0_attn_W_v"]),
        "wo": a(inputs["L0_attn_W_o"]),
        "wdown": a(inputs["L0_attn_W_down_kv"]),
        "wupk": a(inputs["L0_attn_W_up_k"]),
        "wupv": a(inputs["L0_attn_W_up_v"]),
        "gwT": a(np.asarray(inputs["L0_ffn_gate_w"], dtype=f).T),
        "gbias": a(np.asarray(inputs["L0_ffn_gate_bias"],
                              dtype=f).reshape(1, E)),
        "gpT": a(np.asarray(inputs["L0_ffn_gate_proj"],
                            dtype=f).transpose(0, 2, 1)),
        "upT": a(np.asarray(inputs["L0_ffn_up_proj"],
                            dtype=f).transpose(0, 2, 1)),
        "dpT": a(np.asarray(inputs["L0_ffn_down_proj"],
                            dtype=f).transpose(0, 2, 1)),
        "sgT": a(np.asarray(inputs["L0_ffn_shared_gate_w"], dtype=f).T),
        "suT": a(np.asarray(inputs["L0_ffn_shared_up_w"], dtype=f).T),
        "sdT": a(np.asarray(inputs["L0_ffn_shared_down_w"], dtype=f).T),
    }
    t = np.arange(S, dtype=f)
    w["decay"] = np.exp(-LAM * np.abs(t[:, None] - t[None, :])).astype(f)
    w["tinyc"] = (-(np.arange(E, dtype=f) + 1) * 1e-30).reshape(1, E)
    return w


def kernel(**inputs):
    nc = _get_prog(3)
    w = _stage_weights(inputs)
    x = np.asarray(inputs["L0_attn_x"], dtype=np.float32)
    in_maps = []
    for c in range(NCORES):
        m = dict(w)
        m["x"] = np.ascontiguousarray(
            x[c * B_PER_CORE:(c + 1) * B_PER_CORE].reshape(T, D))
        in_maps.append(m)
    res = run_bass_kernel_spmd(nc, in_maps, core_ids=list(range(NCORES)))
    out = np.concatenate(
        [r["out"].reshape(B_PER_CORE, S, D) for r in res.results], axis=0)
    return out


# version 09
# speedup vs baseline: 1.1831x; 1.1122x over previous
import numpy as np
import ml_dtypes
from contextlib import ExitStack

import concourse.bass as bass
import concourse.mybir as mybir
from concourse.bass_utils import run_bass_kernel_spmd
from concourse.alu_op_type import AluOpType as OP

F32 = mybir.dt.float32
BF16 = mybir.dt.bfloat16
AF = mybir.ActivationFunctionType
AX = mybir.AxisListType

NCORES = 8
B_PER_CORE = 2
S = 2048
D = 1024
H = 16
HD = 64
DL = 256
E = 16
I = 1408
NI = 11
IS = 2816
T = B_PER_CORE * S
NT128 = T // 128
NCH = D // 128
LAM = 0.01
EPS = 1e-5
INV_SQRT_D = 1.0 / 32.0
NTSB = 4
TSB = T // NTSB


class Prog:
    ENGINES = ("sync", "scalar", "vector", "tensor", "gpsimd")

    def __init__(self, nc, stack):
        self.nc = nc
        self.stack = stack
        self.sched = []
        self.sems = {}
        self.satisfied = {e: {} for e in self.ENGINES}
        self.n_ops = {e: 0 for e in self.ENGINES}

    def sem(self, name):
        if name not in self.sems:
            h = self.stack.enter_context(self.nc.semaphore(name))
            self.sems[name] = [h, 0]
        return self.sems[name]

    def count(self, name):
        return self.sems[name][1] if name in self.sems else 0

    def op(self, eng, fn=None, waits=(), inc=None, dma=False):
        wl = []
        for w in waits:
            name, val = (w, None) if isinstance(w, str) else w
            if name not in self.sems:
                continue
            s = self.sems[name]
            v = s[1] if val is None else val
            assert v <= s[1], f"wait {name}>={v} exceeds scheduled {s[1]}"
            if v <= 0 or self.satisfied[eng].get(name, 0) >= v:
                continue
            self.satisfied[eng][name] = v
            wl.append((s[0], v))
        incs = None
        if inc is not None:
            s = self.sem(inc)
            k = 16 if dma else 1
            s[1] += k
            incs = (s[0], k)
        self.sched.append((eng, wl, fn, incs))
        self.n_ops[eng] += 1

    def barrier(self):
        names = list(self.sems.keys())
        for e in self.ENGINES:
            self.op(e, None, waits=names)

    def emit(self, block):
        decos = {"sync": block.sync, "scalar": block.scalar,
                 "vector": block.vector, "tensor": block.tensor,
                 "gpsimd": block.gpsimd}

        def make_body(name):
            def body(eng):
                for (en, wl, fn, incs) in self.sched:
                    if en != name:
                        continue
                    for (h, v) in wl:
                        eng.wait_ge(h, v)
                    if fn is None:
                        continue
                    r = fn(eng)
                    if incs is not None:
                        r.then_inc(incs[0], incs[1])
            return body
        for name in self.ENGINES:
            decos[name](make_body(name))


def cstep(e, ch, inst):
    inst.then_inc(ch[0], 1)
    ch[1] += 1
    e.wait_ge(ch[0], ch[1])
    return inst


DEV_DUMPS = False
DEV_NEXPERTS = E + 2


def build_program(dev_phase=3):
    nc = bass.Bass("TRN2", target_bir_lowering=False, debug=False)
    dk = "ExternalOutput" if dev_phase < 3 else "Internal"

    d = {}
    d["x"] = nc.dram_tensor("x", [T, D], F32, kind="ExternalInput").ap()
    d["decay"] = nc.dram_tensor("decay", [S, S], BF16,
                                kind="ExternalInput").ap()
    for nm, shape in [("wq", [D, D]), ("wk", [D, D]), ("wv", [D, D]),
                      ("wo", [D, D]), ("wdown", [D, DL]), ("wupk", [DL, D]),
                      ("wupv", [DL, D]), ("gwT", [D, E]),
                      ("gbias", [1, E]), ("tinyc", [1, E]),
                      ("gpT", [E, D, I]), ("upT", [E, D, I]),
                      ("dpT", [E, I, D]), ("sgT", [D, IS]), ("suT", [D, IS]),
                      ("sdT", [IS, D])]:
        d[nm] = nc.dram_tensor(nm, shape, F32, kind="ExternalInput").ap()
    d["xbf"] = nc.dram_tensor("xbf_s", [T, D], BF16, kind=dk).ap()
    d["qf"] = nc.dram_tensor("qf_s", [D, T], BF16, kind=dk).ap()
    d["kf"] = nc.dram_tensor("kf_s", [D, T], BF16, kind=dk).ap()
    d["c0b"] = nc.dram_tensor("c0b_s", [32], F32, kind=dk).ap()
    if dev_phase == 1:
        d["nrm"] = nc.dram_tensor("nrm_s", [8, 2, 16], F32,
                                  kind="ExternalOutput").ap()
        d["dbg"] = nc.dram_tensor("dbg_s", [32, 512], F32,
                                  kind="ExternalOutput").ap()
    yk = dk if dev_phase == 2 else ("Internal" if dev_phase >= 3 else dk)
    d["y1b"] = nc.dram_tensor("y1b_s", [T, D], BF16, kind=yk).ap()
    d["y1f"] = nc.dram_tensor("y1f_s", [T, D], F32, kind=yk).ap()
    d["out"] = nc.dram_tensor("out", [T, D], F32, kind="ExternalOutput").ap()
    if DEV_DUMPS and dev_phase >= 3:
        d["actd"] = nc.dram_tensor("actd_s", [128, NI, TSB], BF16,
                                   kind="ExternalOutput").ap()
        d["sgd"] = nc.dram_tensor("sgd_s", [128, NI, TSB], BF16,
                                  kind="ExternalOutput").ap()
        d["cwd"] = nc.dram_tensor("cwd_s", [NTSB, 128, TSB // 128, E], F32,
                                  kind="ExternalOutput").ap()
        d["yad"] = nc.dram_tensor("yad_s", [NTSB, 128, TSB // 128, D], F32,
                                  kind="ExternalOutput").ap()
        d["scrd"] = nc.dram_tensor("scrd_s", [NTSB, 128, 128], F32,
                                   kind="ExternalOutput").ap()

    with ExitStack() as st:
        P = Prog(nc, st)
        g = {}
        g["chV"] = [st.enter_context(nc.semaphore("chV")), 0]
        g["chS"] = [st.enter_context(nc.semaphore("chS")), 0]
        g["ones"] = st.enter_context(nc.sbuf_tensor("g_ones", [1, 128], F32))
        g["c0neg"] = st.enter_context(nc.sbuf_tensor("g_c0neg", [128, 1], F32))
        g["qnmax"] = st.enter_context(nc.sbuf_tensor("g_qnmax", [16, 1], F32))
        g["knmax"] = st.enter_context(nc.sbuf_tensor("g_knmax", [16, 1], F32))
        g["ind"] = st.enter_context(nc.sbuf_tensor("g_ind", [128, NCH, 16],
                                                   BF16))
        g["r32"] = st.enter_context(nc.sbuf_tensor("g_r32", [1, 32], F32))
        g["t11"] = st.enter_context(nc.sbuf_tensor("g_t11", [1, 4], F32))

        P.op("vector", lambda e: e.memset(g["ones"][:], 1.0), inc="init")
        P.op("vector", lambda e: e.memset(g["ind"][:], 0.0), inc="init")
        for co in range(NCH):
            P.op("vector", lambda e, co=co: e.memset(
                g["ind"][0:64, co, 2 * co:2 * co + 1], 1.0), inc="init")
            P.op("vector", lambda e, co=co: e.memset(
                g["ind"][64:128, co, 2 * co + 1:2 * co + 2], 1.0), inc="init")
        P.op("vector", lambda e: e.memset(g["qnmax"][:], 0.0), inc="init")
        P.op("vector", lambda e: e.memset(g["knmax"][:], 0.0), inc="init")

        with ExitStack() as ab:
            vaug = ab.enter_context(
                nc.sbuf_tensor("vaug", [128, NT128, H, HD + 1], BF16))
            P.op("vector", lambda e: e.memset(vaug[:], 1.0), inc="init")
            with ExitStack() as a_st:
                emit_phase_A(nc, P, a_st, d, g, vaug)
            P.barrier()
            if dev_phase >= 2:
                with ExitStack() as b_st:
                    emit_phase_B(nc, P, b_st, d, g, vaug)
                P.barrier()

        if dev_phase >= 3:
            with ExitStack() as fs:
                emit_phase_F(nc, P, fs, d, g)
        elif dev_phase == 2:
            bufo = st.enter_context(nc.sbuf_tensor("bufo", [128, D], F32))
            for i in range(NT128):
                P.op("sync", lambda e, i=i: e.dma_start(
                    bufo[:], d["y1f"][i * 128:(i + 1) * 128, :]),
                    waits=["y1st0", "y1st1", ("outst", None)],
                    inc="devld", dma=True)
                P.op("sync", lambda e, i=i: e.dma_start(
                    d["out"][i * 128:(i + 1) * 128, :], bufo[:]),
                    waits=["devld"], inc="outst", dma=True)
        else:
            bufo = st.enter_context(nc.sbuf_tensor("bufo", [128, D], F32))
            P.op("vector", lambda e: e.memset(bufo[:], 0.0), inc="z")
            for i in range(NT128):
                P.op("sync", lambda e, i=i: e.dma_start(
                    d["out"][i * 128:(i + 1) * 128, :], bufo[:]),
                    waits=["z"], inc="outst", dma=True)

        P.barrier()
        with nc.Block() as block:
            P.emit(block)

    print(f"[build] ops per engine: {P.n_ops} "
          f"total={sum(P.n_ops.values())} sems={len(P.sems)}")
    return nc


ab_dbg = []


def emit_phase_A(nc, P, ab, d, g, vaug):
    sbA = lambda name, shape, dt=BF16: ab.enter_context(
        nc.sbuf_tensor(name, shape, dt))
    psA = lambda name, shape: ab.enter_context(
        nc.psum_tensor(name, shape, F32))

    w_wq = sbA("w_wq", [128, NCH, D])
    w_wk = sbA("w_wk", [128, NCH, D])
    w_wv = sbA("w_wv", [128, NCH, D])
    w_wd = sbA("w_wd", [128, NCH, DL])
    w_uk = sbA("w_uk", [128, 2, D])
    w_uv = sbA("w_uv", [128, 2, D])
    xbb = [sbA(f"xbb{p}", [128, 4, D]) for p in range(2)]
    xfm = [sbA(f"xfm{p}", [128, NCH, 512]) for p in range(2)]
    kv_sb = sbA("kv_sb", [128, NCH, 512])
    lat_sb = sbA("lat_sb", [128, 2, 512])
    q_sb = sbA("q_sb", [128, NCH, 512])
    k_sb = sbA("k_sb", [128, NCH, 512])
    sqq = [sbA(f"sqq{p}", [128, 512]) for p in range(2)]
    tmx = sbA("tmx", [16, 1], F32)
    c0one = sbA("c0one", [1, 1], F32)

    ab_dbg.clear()
    if "dbg" in d:
        ab_dbg.append(sbA("dbg_sb", [64, 512], F32))
    pQ = psA("pQ", [128, 512])
    pK = psA("pK", [128, 512])
    pV = psA("pV", [128, 512])
    pVT = psA("pVT", [128, 1024])
    pNQ = psA("pNQ", [16, 512])
    pNK = psA("pNK", [16, 512])
    pC0 = psA("pC0", [128, 1])

    for dst, src in [(w_wq, d["wq"]), (w_wk, d["wk"]), (w_wv, d["wv"]),
                     (w_wd, d["wdown"]), (w_uk, d["wupk"]), (w_uv, d["wupv"])]:
        P.op("gpsimd", lambda e, dst=dst, src=src: e.dma_start(
            dst[:], src.rearrange("(c p) o -> p c o", p=128)),
            inc="wA", dma=True)

    xfm_guard = {0: 0, 1: 0}
    for tb in range(8):
        t0 = tb * 512
        par = tb % 2
        P.op("gpsimd", lambda e, t0=t0, par=par: e.dma_start(
            xbb[par][:],
            d["x"][t0:t0 + 512, :].rearrange("(i p) f -> p i f", p=128)),
            waits=[(f"xst{par}", None)], inc=f"xcast{par}", dma=True)
        P.op("sync", lambda e, t0=t0, par=par: e.dma_start(
            d["xbf"][t0:t0 + 512, :].rearrange("(i p) f -> p i f", p=128),
            xbb[par][:]),
            waits=[f"xcast{par}"], inc=f"xst{par}", dma=True)
        P.op("scalar", lambda e, t0=t0, par=par: e.dma_start(
            xfm[par][:], d["xbf"][t0:t0 + 512, :], transpose=True),
            waits=[f"xst{par}", ("pe_v", xfm_guard[par])],
            inc=f"xfm{par}", dma=True)

        for co in range(NCH):
            def qchain(e, co=co, par=par):
                r = None
                for ci in range(NCH):
                    r = e.matmul(pQ[:], w_wq[:, ci, co * 128:(co + 1) * 128],
                                 xfm[par][:, ci, :],
                                 start=(ci == 0), stop=(ci == NCH - 1))
                return r
            P.op("tensor", qchain,
                 waits=["wA", f"xfm{par}", ("evq", None), ("latcp", None)],
                 inc="pe_q")
            P.op("vector", lambda e, co=co: e.tensor_scalar(
                q_sb[:, co, :], pQ[:], INV_SQRT_D, None, OP.mult),
                waits=[("pe_q", None), ("qst", None), ("sq_q", None)],
                inc="evq")
            sp = co % 2
            P.op("vector", lambda e, co=co, sp=sp: e.tensor_mul(
                sqq[sp][:], q_sb[:, co, :], q_sb[:, co, :]),
                waits=[("evq", None), ("pe_nq", None), ("pe_nk", None)],
                inc="sq_q")
            P.op("tensor", lambda e, co=co, sp=sp: e.matmul(
                pNQ[:], g["ind"][:, co, :], sqq[sp][:],
                start=(co == 0), stop=(co == NCH - 1)),
                waits=[("sq_q", None), ("nqred", None), "init"], inc="pe_nq")

            def kchain(e, co=co, par=par):
                r = None
                for ci in range(NCH):
                    r = e.matmul(pK[:], w_wk[:, ci, co * 128:(co + 1) * 128],
                                 xfm[par][:, ci, :],
                                 start=(ci == 0), stop=(ci == NCH - 1))
                return r
            P.op("tensor", kchain, waits=[("kcp", None), ("krcp", None)],
                 inc="pe_k")
            P.op("scalar", lambda e, co=co: e.copy(kv_sb[:, co, :], pK[:]),
                 waits=[("pe_k", None), ("pe_lat", None)], inc="kcp")

            def vchain(e, co=co, par=par):
                r = None
                for ci in range(NCH):
                    r = e.matmul(pV[:], w_wv[:, ci, co * 128:(co + 1) * 128],
                                 xfm[par][:, ci, :],
                                 start=(ci == 0), stop=(ci == NCH - 1))
                return r
            P.op("tensor", vchain, waits=[("kvadd", None)], inc="pe_v")
            P.op("vector", lambda e, co=co: e.tensor_tensor(
                kv_sb[:, co, :], pV[:], kv_sb[:, co, :], OP.add),
                waits=[("pe_v", None), ("kcp", None)], inc="kvadd")
        xfm_guard[par] = P.count("pe_v")

        for lco in range(2):
            pl = pQ if lco == 0 else pK
            def latchain(e, lco=lco, pl=pl):
                r = None
                for ci in range(NCH):
                    r = e.matmul(pl[:], w_wd[:, ci, lco * 128:(lco + 1) * 128],
                                 kv_sb[:, ci, :],
                                 start=(ci == 0), stop=(ci == NCH - 1))
                return r
            P.op("tensor", latchain,
                 waits=[("kvadd", None), ("latcp", None), ("evq", None),
                        ("krcp", None)],
                 inc="pe_lat")
            P.op("scalar", lambda e, lco=lco, pl=pl: e.copy(
                lat_sb[:, lco, :], pl[:]),
                waits=[("pe_lat", None), ("pe_krec", None), ("pe_vt", None)],
                inc="latcp")

        kr_guard = {0: 0, 1: 0}
        for co in range(NCH):
            pk2 = pK if co % 2 == 0 else pV
            def krchain(e, co=co, pk2=pk2):
                r = None
                for ci in range(2):
                    r = e.matmul(pk2[:], w_uk[:, ci, co * 128:(co + 1) * 128],
                                 lat_sb[:, ci, :],
                                 start=(ci == 0), stop=(ci == 1))
                return r
            P.op("tensor", krchain,
                 waits=[("latcp", None), ("krcp", kr_guard[co % 2]),
                        ("kcp", None), ("kvadd", None)],
                 inc="pe_krec")
            P.op("scalar", lambda e, co=co, pk2=pk2: e.copy(
                k_sb[:, co, :], pk2[:]),
                waits=[("pe_krec", None), ("kst", None), ("sq_k", None)],
                inc="krcp")
            kr_guard[co % 2] = P.count("krcp")
            sp = co % 2
            P.op("vector", lambda e, co=co, sp=sp: e.tensor_mul(
                sqq[sp][:], k_sb[:, co, :], k_sb[:, co, :]),
                waits=[("krcp", None), ("pe_nk", None), ("pe_nq", None)],
                inc="sq_k")
            P.op("tensor", lambda e, co=co, sp=sp: e.matmul(
                pNK[:], g["ind"][:, co, :], sqq[sp][:],
                start=(co == 0), stop=(co == NCH - 1)),
                waits=[("sq_k", None), ("nkred", None)], inc="pe_nk")

        for i in range(4):
            def vtchain(e, i=i):
                r = None
                for n in range(2):
                    for ci in range(2):
                        r = e.matmul(
                            pVT[:, n * 512:(n + 1) * 512],
                            lat_sb[:, ci, i * 128:(i + 1) * 128],
                            w_uv[:, ci, n * 512:(n + 1) * 512],
                            start=(ci == 0), stop=(ci == 1))
                return r
            P.op("tensor", vtchain, waits=[("latcp", None), ("vev", None)],
                 inc="pe_vt")
            gi = tb * 4 + i
            P.op("vector", lambda e, gi=gi: e.tensor_copy(
                vaug[:, gi, :, 0:HD],
                pVT.ap().rearrange("p (h v) -> p h v", v=64)),
                waits=[("pe_vt", None), "init"], inc="vev")

        P.op("vector", lambda e: e.reduce_max(tmx[:], pNQ[:], axis=AX.X),
             waits=[("pe_nq", None), ("nqmax", None)], inc="nqred")
        P.op("vector", lambda e: e.tensor_tensor(
            g["qnmax"][:], tmx[:], g["qnmax"][:], OP.max),
            waits=["init", ("nqred", None), ("nqmax", None)], inc="nqmax")
        P.op("vector", lambda e: e.reduce_max(tmx[:], pNK[:], axis=AX.X),
             waits=[("pe_nk", None), ("nqmax", None), ("nkmax", None)],
             inc="nkred")
        P.op("vector", lambda e: e.tensor_tensor(
            g["knmax"][:], tmx[:], g["knmax"][:], OP.max),
            waits=[("nkred", None), ("nkmax", None)], inc="nkmax")

        P.op("sync", lambda e, t0=t0: e.dma_start(
            d["qf"].rearrange("(c p) t -> p c t", p=128)[:, :, t0:t0 + 512],
            q_sb[:]), waits=[("evq", None)], inc="qst", dma=True)
        P.op("sync", lambda e, t0=t0: e.dma_start(
            d["kf"].rearrange("(c p) t -> p c t", p=128)[:, :, t0:t0 + 512],
            k_sb[:]), waits=[("krcp", None)], inc="kst", dma=True)
        if "nrm" in d:
            P.op("sync", lambda e, tb=tb: e.dma_start(
                d["nrm"][tb, 0, :], g["qnmax"][:]),
                waits=[("nqmax", None)], inc="nrmst", dma=True)
            P.op("sync", lambda e, tb=tb: e.dma_start(
                d["nrm"][tb, 1, :], g["knmax"][:]),
                waits=[("nkmax", None)], inc="nrmst", dma=True)
        if "dbg" in d and tb == 0:
            dbg = ab_dbg[0]
            P.op("vector", lambda e: e.tensor_copy(dbg[0:16, :], pNQ[:]),
                 waits=[("pe_nq", None)], inc="dbgv")
            P.op("vector", lambda e: e.tensor_copy(dbg[32:48, :], pNK[:]),
                 waits=[("pe_nk", None)], inc="dbgv")
            P.op("sync", lambda e: e.dma_start(d["dbg"][0:16, :],
                                               dbg[0:16, :]),
                 waits=["dbgv"], inc="dbgst", dma=True)
            P.op("sync", lambda e: e.dma_start(d["dbg"][16:32, :],
                                               dbg[32:48, :]),
                 waits=["dbgv"], inc="dbgst", dma=True)

    P.op("sync", lambda e: e.dma_start(d["c0b"][0:16], g["qnmax"][:]),
         waits=[("nqmax", None)], inc="c0st", dma=True)
    P.op("sync", lambda e: e.dma_start(d["c0b"][16:32], g["knmax"][:]),
         waits=[("nkmax", None)], inc="c0st", dma=True)
    P.op("sync", lambda e: e.dma_start(
        g["r32"][:], d["c0b"].rearrange("(o a) -> o a", o=1)),
        waits=["c0st"], inc="c0ld", dma=True)
    P.op("vector", lambda e: e.reduce_max(
        g["t11"][:, 0:1], g["r32"][:, 0:16], axis=AX.X),
        waits=["c0ld"], inc="c0v")
    P.op("vector", lambda e: e.reduce_max(
        g["t11"][:, 1:2], g["r32"][:, 16:32], axis=AX.X), inc="c0v")
    P.op("vector", lambda e: e.tensor_tensor(
        g["t11"][:, 2:3], g["t11"][:, 0:1], g["t11"][:, 1:2], OP.mult),
        waits=[("c0v", None)], inc="c0v")
    P.op("scalar", lambda e: e.activation(
        g["t11"][:, 3:4], g["t11"][:, 2:3], AF.Ln),
        waits=[("c0v", None)], inc="c0a")
    P.op("scalar", lambda e: e.activation(
        c0one[:], g["t11"][:, 3:4], AF.Exp, scale=0.5),
        waits=[("c0a", None)], inc="c0a")
    P.op("vector", lambda e: e.tensor_scalar(
        c0one[:], c0one[:], -1.0, None, OP.mult),
        waits=[("c0a", None)], inc="c0n")
    P.op("tensor", lambda e: e.matmul(
        pC0[:], g["ones"][:, 0:128], c0one[:], start=True, stop=True),
        waits=[("c0n", None)], inc="c0mm")
    P.op("scalar", lambda e: e.copy(g["c0neg"][:], pC0[:]),
         waits=[("c0mm", None)], inc="c0done")


def emit_phase_B(nc, P, ab, d, g, vaug):
    sbB = lambda name, shape, dt=BF16: ab.enter_context(
        nc.sbuf_tensor(name, shape, dt))
    psB = lambda name, shape: ab.enter_context(
        nc.psum_tensor(name, shape, F32))

    w_wo = sbB("w_wo", [128, NCH, D])
    Ksb = sbB("Ksb", [128, NCH, S])
    Qsb = sbB("Qsb", [128, NCH, 512])
    dec = sbB("dec", [128, 16, 512], BF16)
    esb = [sbB(f"esb{p}", [128, 512], F32) for p in range(6)]
    PT8 = sbB("PT8", [128, 8, 512])
    _l = sbB("l_sb", [1, 512], F32)
    _r = sbB("r_sb", [1, 512], F32)
    l_sb = [_l, _l]
    r_sb = [_r, _r]
    _rb = sbB("rb_sb", [64, 512], F32)
    rb_sb = [_rb, _rb]
    o_all = sbB("o_all", [128, NCH, 512])
    xres = [sbB(f"xres{p}", [128, D], F32) for p in range(2)]
    _xa = sbB("xa", [128, D], F32)
    xa = [_xa, _xa]
    _y1fb = sbB("y1fb", [128, D], F32)
    y1fb = [_y1fb, _y1fb]
    y1bb = [sbB(f"y1bb{p}", [128, D]) for p in range(2)]
    stat = [sbB(f"stat{p}", [128, 8], F32) for p in range(2)]

    pSq = [psB(f"pSq{p}", [128, 512]) for p in range(4)]
    pO = [psB(f"pO{p}", [HD + 1, 512]) for p in range(2)]
    pAO = psB("pAO", [128, 1024])

    P.op("gpsimd", lambda e: e.dma_start(
        w_wo[:], d["wo"].rearrange("(c p) o -> p c o", p=128)),
        inc="wold", dma=True)

    gd = {"tt": [0, 0], "ex": [0, 0], "pv_pt": [0, 0],
          "lcp": [0, 0], "oev": [0, 0], "rbc": [0, 0]}

    for b in range(B_PER_CORE):
        P.op("sync", lambda e, b=b: e.dma_start(
            Ksb[:],
            d["kf"][:, b * S:(b + 1) * S].rearrange("(c p) t -> p c t",
                                                    p=128)),
            waits=[("pe_sc", None)], inc="Kld", dma=True)
        for qh in range(4):
            q0 = b * S + qh * 512
            P.op("sync", lambda e, q0=q0: e.dma_start(
                Qsb[:],
                d["qf"][:, q0:q0 + 512].rearrange("(c p) t -> p c t", p=128)),
                waits=[("pe_sc", None)], inc="Qld", dma=True)
            P.op("sync", lambda e, qh=qh: e.dma_start(
                dec[:],
                d["decay"][:, qh * 512:(qh + 1) * 512]
                .rearrange("(c p) q -> p c q", p=128)),
                waits=[("tt", None)], inc="decld", dma=True)
            for i in range(2):
                pr = i % 2
                P.op("sync", lambda e, q0=q0, i=i, pr=pr: e.dma_start(
                    xres[pr][:], d["x"][q0 + i * 128:q0 + (i + 1) * 128, :]),
                    waits=[(f"lnadd{pr}", None)], inc=f"xres{pr}", dma=True)

            cnt = {"tt": {}, "ex": {}, "pv": {}}

            def emit_pv(s, hh):
                hp, c = s // 16, s % 16
                h = 2 * hp + hh
                po = hh
                gi = b * 16
                w = [("ex", cnt["ex"][(s, hh)])]
                if c == 0:
                    w += [("lcp", gd["lcp"][po]), ("oev", gd["oev"][po])]
                sl = (s % 4) * 2 + hh
                P.op("tensor", lambda e, gi=gi, h=h, c=c, po=po, sl=sl:
                     e.matmul(pO[po][:], vaug[:, gi + c, h, :],
                              PT8[:, sl, :], start=(c == 0), stop=(c == 15)),
                     waits=w, inc="pe_pv")
                cnt["pv"][(s, hh)] = P.count("pe_pv")
                if c == 15:
                    emit_evac(h, po)

            def emit_evac(h, po):
                ch2, hb2 = h // 2, (h % 2) * 64
                P.op("scalar", lambda e, po=po: e.copy(
                    l_sb[po][:], pO[po][64:65, :]),
                    waits=[("pe_pv", None)], inc="lcp")
                P.op("scalar", lambda e, po=po: e.activation(
                    r_sb[po][:], l_sb[po][:], AF.Ln),
                    waits=[("lcp", None), ("pe_rb", None)], inc="lcp")
                P.op("scalar", lambda e, po=po: e.activation(
                    r_sb[po][:], r_sb[po][:], AF.Exp, scale=-1.0),
                    waits=[("lcp", None)], inc="lcp")
                gd["lcp"][po] = P.count("lcp")
                P.op("tensor", lambda e, po=po: e.matmul(
                    pAO[0:64, 0:512], g["ones"][:, 0:64], r_sb[po][:],
                    start=True, stop=True),
                    waits=[("lcp", None), ("rbc", None),
                           ("lnadd0", None), ("lnadd1", None)], inc="pe_rb")
                P.op("scalar", lambda e, po=po: e.copy(
                    rb_sb[po][:], pAO[0:64, 0:512]),
                    waits=[("pe_rb", None), ("oev", None)], inc="rbc")
                gd["rbc"][po] = P.count("rbc")
                P.op("vector",
                     lambda e, ch2=ch2, hb2=hb2, po=po: e.tensor_tensor(
                         o_all[hb2:hb2 + 64, ch2, :], pO[po][0:64, :],
                         rb_sb[po][:], OP.mult),
                     waits=[("rbc", None), ("pe_wo", None)], inc="oev")
                gd["oev"][po] = P.count("oev")

            for s in range(128):
                hp, c = s // 16, s % 16
                cp = c % 2

                def smm2(e, hp=hp, c=c, cp=cp):
                    e.matmul(pSq[cp][:], Ksb[0:64, hp, c * 128:c * 128 + 128],
                             Qsb[0:64, hp, :], start=True, stop=True)
                    return e.matmul(pSq[2 + cp][:],
                                    Ksb[64:128, hp, c * 128:c * 128 + 128],
                                    Qsb[64:128, hp, :],
                                    start=True, stop=True)
                P.op("tensor", smm2,
                     waits=["Kld", "Qld", "c0done",
                            ("tt", cnt["tt"].get((s - 2, 1), 0))],
                     inc="pe_sc")
                for hh in range(2):
                    esl = (s % 3) * 2 + hh
                    P.op("vector", lambda e, c=c, cp=cp, hh=hh, esl=esl:
                         e.tensor_tensor(esb[esl][:],
                                         pSq[hh * 2 + cp][:],
                                         dec[:, c, :], OP.mult),
                         waits=[("pe_sc", None), "decld",
                                ("ex", cnt["ex"].get((s - 3, hh), 0))],
                         inc="tt")
                    cnt["tt"][(s, hh)] = P.count("tt")
                    sl = (s % 4) * 2 + hh
                    P.op("scalar", lambda e, hh=hh, sl=sl, esl=esl:
                         e.activation(PT8[:, sl, :], esb[esl][:],
                                      AF.Exp, bias=g["c0neg"][:], scale=1.0),
                         waits=[("tt", None),
                                ("pe_pv", cnt["pv"].get((s - 4, hh), 0))],
                         inc="ex")
                    cnt["ex"][(s, hh)] = P.count("ex")
                if s >= 2:
                    emit_pv(s - 2, 0)
                    emit_pv(s - 2, 1)
            for s in (126, 127):
                emit_pv(s, 0)
                emit_pv(s, 1)

            for i in range(4):
                pr = i % 2
                tg = q0 + i * 128

                def wochain(e, i=i):
                    r = None
                    for n in range(2):
                        for ci in range(NCH):
                            r = e.matmul(
                                pAO[:, n * 512:(n + 1) * 512],
                                o_all[:, ci, i * 128:(i + 1) * 128],
                                w_wo[:, ci, n * 512:(n + 1) * 512],
                                start=(ci == 0), stop=(ci == NCH - 1))
                    return r
                P.op("tensor", wochain,
                     waits=[("oev", None), "wold", ("rbc", None),
                            ("lnadd0", None), ("lnadd1", None)], inc="pe_wo")
                P.op("vector", lambda e, pr=pr: e.tensor_tensor(
                    xa[pr][:], pAO[:], xres[pr][:], OP.add),
                    waits=[("pe_wo", None), (f"xres{pr}", None),
                           ("y1w0", None), ("y1w1", None), ("lnsq", None),
                           ("rstd0", None), ("rstd1", None)],
                    inc=f"lnadd{pr}")
                P.op("vector", lambda e, pr=pr: e.reduce_sum(
                    stat[pr][:, 0:1], xa[pr][:], axis=AX.X),
                    waits=[(f"lnadd{pr}", None)], inc=f"lnst{pr}")
                P.op("scalar", lambda e, pr=pr: e.activation(
                    y1bb[pr][:], xa[pr][:], AF.Square,
                    accum_out=stat[pr][:, 1:2]),
                    waits=[(f"lnadd{pr}", None), ("lnsq", None),
                           (f"y1st{pr}", None)], inc="lnsq")

                def lnchain(e, pr=pr):
                    ch = g["chV"]
                    cstep(e, ch, e.tensor_scalar(
                        stat[pr][:, 2:3], stat[pr][:, 0:1], 1.0 / D, None,
                        OP.mult))
                    cstep(e, ch, e.tensor_scalar(
                        stat[pr][:, 3:4], stat[pr][:, 1:2], 1.0 / D, None,
                        OP.mult))
                    cstep(e, ch, e.tensor_mul(
                        stat[pr][:, 4:5], stat[pr][:, 2:3],
                        stat[pr][:, 2:3]))
                    cstep(e, ch, e.tensor_sub(
                        stat[pr][:, 5:6], stat[pr][:, 3:4],
                        stat[pr][:, 4:5]))
                    return e.tensor_scalar(stat[pr][:, 5:6],
                                           stat[pr][:, 5:6], EPS, None,
                                           OP.add)
                P.op("vector", lnchain,
                     waits=[("lnsq", None), (f"lnst{pr}", None)],
                     inc=f"lnst{pr}")

                def rstdchain(e, pr=pr):
                    cstep(e, g["chS"], e.activation(
                        stat[pr][:, 6:7], stat[pr][:, 5:6], AF.Ln))
                    return e.activation(stat[pr][:, 7:8], stat[pr][:, 6:7],
                                        AF.Exp, scale=-0.5)
                P.op("scalar", rstdchain, waits=[(f"lnst{pr}", None)],
                     inc=f"rstd{pr}")
                P.op("vector", lambda e, pr=pr: e.tensor_scalar(
                    y1fb[pr][:], xa[pr][:], stat[pr][:, 2:3],
                    stat[pr][:, 7:8], OP.subtract, OP.mult),
                    waits=[(f"rstd{pr}", None), ("y1st0", None),
                           ("y1st1", None)],
                    inc=f"y1w{pr}")
                P.op("vector", lambda e, pr=pr: e.tensor_scalar(
                    y1bb[pr][:], xa[pr][:], stat[pr][:, 2:3],
                    stat[pr][:, 7:8], OP.subtract, OP.mult),
                    inc=f"y1w{pr}")
                P.op("sync", lambda e, tg=tg, pr=pr: e.dma_start(
                    d["y1f"][tg:tg + 128, :], y1fb[pr][:]),
                    waits=[(f"y1w{pr}", None)], inc=f"y1st{pr}", dma=True)
                P.op("sync", lambda e, tg=tg, pr=pr: e.dma_start(
                    d["y1b"][tg:tg + 128, :], y1bb[pr][:]),
                    inc=f"y1st{pr}", dma=True)
                if i + 2 < 4:
                    P.op("sync", lambda e, q0=q0, i=i, pr=pr: e.dma_start(
                        xres[pr][:],
                        d["x"][q0 + (i + 2) * 128:q0 + (i + 3) * 128, :]),
                        waits=[(f"lnadd{pr}", None)], inc=f"xres{pr}",
                        dma=True)


def emit_phase_F(nc, P, fs, d, g):
    sbF = lambda name, shape, dt=BF16: fs.enter_context(
        nc.sbuf_tensor(name, shape, dt))
    psF = lambda name, shape: fs.enter_context(
        nc.psum_tensor(name, shape, F32))

    ring = [sbF(f"ring{s}", [128, NCH, 512]) for s in range(4)]
    dpb = [sbF(f"dpb{p}", [128, NI, D]) for p in range(2)]
    sg_sb = sbF("sg_sb", [128, NI, TSB])
    act_sb = sbF("act_sb", [128, NI, TSB])
    y1fm = [sbF(f"y1fm{p}", [128, NCH, TSB]) for p in range(2)]
    yacc = sbF("yacc", [128, TSB // 128, D], F32)
    gw_sb = sbF("gw_sb", [128, NCH, E])
    gb_sb = sbF("gb_sb", [1, E], F32)
    tn_sb = sbF("tn_sb", [1, E], F32)
    gbb = sbF("gbb", [128, E], F32)
    tnb = sbF("tnb", [128, E], F32)
    scr = [sbF(f"scr{p}", [128, 128], F32) for p in range(2)]
    cw = [sbF(f"cw{p}", [128, TSB // 128, E], F32) for p in range(2)]
    gwk = sbF("gwk", [128, 96], F32)
    y1tm = [sbF(f"y1tm{p}", [128, D], F32) for p in range(2)]
    outb = [sbF(f"outb{p}", [128, D], F32) for p in range(2)]
    sqd = sbF("sqdF", [128, D])
    stA = sbF("stA", [128, 8], F32)
    stB = sbF("stB", [128, 8], F32)
    stC = sbF("stC", [128, 8], F32)

    pG = [psF(f"pG{p}", [128, 512]) for p in range(2)]
    pU = [psF(f"pU{p}", [128, 512]) for p in range(2)]
    pY = [psF(f"pY{p}", [128, 1024]) for p in range(2)]

    NT = TSB // 128

    P.op("gpsimd", lambda e: e.dma_start(
        gw_sb[:], d["gwT"].rearrange("(c p) o -> p c o", p=128)),
        inc="gwld", dma=True)
    P.op("sync", lambda e: e.dma_start(gb_sb[:], d["gbias"][:, :]),
         inc="gbld", dma=True)
    P.op("sync", lambda e: e.dma_start(tn_sb[:], d["tinyc"][:, :]),
         inc="gbld", dma=True)
    P.op("tensor", lambda e: e.matmul(pG[0][:, 0:16], g["ones"][:, 0:128],
                                      gb_sb[:], start=True, stop=True),
         waits=["gbld"], inc="pe_gc")
    P.op("tensor", lambda e: e.matmul(pG[1][:, 0:16], g["ones"][:, 0:128],
                                      tn_sb[:], start=True, stop=True),
         inc="pe_gc")
    P.op("scalar", lambda e: e.copy(gbb[:], pG[0][:, 0:16]),
         waits=[("pe_gc", None)], inc="gc")
    P.op("scalar", lambda e: e.copy(tnb[:], pG[1][:, 0:16]), inc="gc")

    slot_state = [0] * 4
    next_slot = [0]
    dp_guard = {0: 0, 1: 0}
    uact_names = ("uact0", "uact1")
    pairs = [(i, j) for i in range(4) for j in range(i + 1, 4)]

    def load_ring(src_ap):
        s = next_slot[0] % 4
        next_slot[0] += 1
        guard = slot_state[s]
        w = src_ap.shape[-1]
        P.op("gpsimd", lambda e, s=s, w=w, src_ap=src_ap: e.dma_start(
            ring[s][:, :, 0:w], src_ap),
            waits=[("pe_gu", guard)], inc=f"ring{s}", dma=True)
        return s

    gate_state = [None]

    def gchain_emit(i8, par):
        def gchain(e, i8=i8, par=par):
            ch = g["chV"]
            cs = lambda inst: cstep(e, ch, inst)
            sc = scr[par][:, i8 * 16:(i8 + 1) * 16]
            sfc = gwk[:, 0:16]
            cs(e.tensor_tensor(sfc, sc, gbb[:], OP.add))
            sfg = sfc.rearrange("p (gg m) -> p gg m", m=4)
            for k, (i, j) in enumerate(pairs):
                cs(e.tensor_tensor(gwk[:, 24 + k * 4:28 + k * 4],
                                   sfg[:, :, i], sfg[:, :, j], OP.add))
            gs = gwk[:, 16:20]
            cs(e.tensor_tensor(gs, gwk[:, 24:28], gwk[:, 28:32], OP.max))
            for k in range(2, 6):
                cs(e.tensor_tensor(gs, gs, gwk[:, 24 + k * 4:28 + k * 4],
                                   OP.max))
            for k, (i, j) in enumerate(pairs):
                cs(e.tensor_tensor(gwk[:, 48 + k:49 + k], gs[:, i:i + 1],
                                   gs[:, j:j + 1], OP.min))
            cs(e.tensor_tensor(gwk[:, 54:55], gwk[:, 48:49], gwk[:, 49:50],
                               OP.max))
            for k in range(2, 6):
                cs(e.tensor_tensor(gwk[:, 54:55], gwk[:, 54:55],
                                   gwk[:, 48 + k:49 + k], OP.max))
            gmask = gwk[:, 20:24]
            cs(e.tensor_scalar(gmask, gs, gwk[:, 54:55], None, OP.is_ge))
            gmb = bass.AP(gmask.tensor, gmask.offset,
                          [gmask.ap[0], [1, 4], [0, 4]])
            tmp = gwk[:, 32:48]
            cs(e.tensor_tensor(tmp, sfc, tnb[:], OP.subtract))
            cs(e.tensor_tensor(tmp, tmp, gmb, OP.mult))
            cs(e.tensor_tensor(tmp, tmp, tnb[:], OP.add))
            w = gwk[:, 0:16]
            cs(e.tensor_copy(w, tmp))
            msk = gwk[:, 64:80]
            mx = gwk[:, 80:81]
            for _ in range(3):
                cs(e.reduce_max(mx, w, axis=AX.X))
                cs(e.tensor_scalar(msk, w, mx, None, OP.is_ge))
                cs(e.scalar_tensor_tensor(w, msk, -1e38, w, OP.mult,
                                          OP.add))
            cs(e.reduce_max(mx, w, axis=AX.X))
            sel = gwk[:, 64:80]
            cs(e.tensor_scalar(sel, tmp, mx, None, OP.is_ge))
            wu = gwk[:, 32:48]
            cs(e.tensor_tensor(wu, sel, sc, OP.mult))
            cs(e.reduce_sum(gwk[:, 81:82], wu, axis=AX.X))
            cs(e.tensor_scalar(gwk[:, 81:82], gwk[:, 81:82], 1e-20, None,
                               OP.add))
            cs(e.reciprocal(gwk[:, 82:83], gwk[:, 81:82]))
            return e.tensor_scalar(cw[par][:, i8, :], wu, gwk[:, 82:83],
                                   None, OP.mult)
        P.op("vector", gchain, waits=[("glog", None), "gc"], inc="gdve")

    def gate_emit(tsb):
        par = tsb % 2
        t0 = tsb * TSB
        P.op("scalar", lambda e, t0=t0, par=par: e.dma_start(
            y1fm[par][:], d["y1b"][t0:t0 + TSB, :], transpose=True),
            waits=[("pe_gu", None), ("pe_glog", None), "y1st0", "y1st1"],
            inc=f"y1fm{par}", dma=True)

        def logits(e, par=par):
            r = None
            for i8 in range(NT):
                for ci in range(NCH):
                    r = e.matmul(pU[0][:, i8 * 16:(i8 + 1) * 16],
                                 y1fm[par][:, ci, i8 * 128:(i8 + 1) * 128],
                                 gw_sb[:, ci, :],
                                 start=(i8 == 0 and ci == 0),
                                 stop=(i8 == NT - 1 and ci == NCH - 1),
                                 skip_group_check=not (
                                     (i8 == 0 and ci == 0)
                                     or (i8 == NT - 1 and ci == NCH - 1)))
            return r
        P.op("tensor", logits,
             waits=[f"y1fm{par}", "gwld", "gc", ("glog", None),
                    ("uact0", None), ("uact1", None)], inc="pe_glog")
        P.op("scalar", lambda e, par=par: e.activation(
            scr[par][:], pU[0][:, 0:128], AF.Sigmoid),
            waits=[("pe_glog", None), ("gdve", None)], inc="glog")

        gate_state[0] = (tsb, par)
        if "cwd" in d:
            P.op("sync", lambda e, tsb=tsb, par=par: e.dma_start(
                d["cwd"][tsb], cw[par][:]), waits=["gdve"], inc="dvst",
                dma=True)
            P.op("sync", lambda e, tsb=tsb, par=par: e.dma_start(
                d["scrd"][tsb], scr[par][:]), waits=["glog"], inc="dvst",
                dma=True)

    def expert_emit(tsb, ee):
        par = tsb % 2
        routed = ee < E
        se = ee - E
        if routed:
            gsrc, usrc, dsrc = d["gpT"][ee], d["upT"][ee], d["dpT"][ee]
        else:
            gsrc = d["sgT"][:, se * I:(se + 1) * I]
            usrc = d["suT"][:, se * I:(se + 1) * I]
            dsrc = d["sdT"][se * I:(se + 1) * I, :]

        widths = [512, 512, 384]
        gslots = [load_ring(gsrc[:, j * 512:j * 512 + w]
                            .rearrange("(c p) i -> p c i", p=128))
                  for j, w in enumerate(widths)]
        dpar = ee % 2
        P.op("gpsimd", lambda e, dsrc=dsrc, dpar=dpar: e.dma_start(
            dpb[dpar][:], dsrc.rearrange("(c p) h -> p c h", p=128)),
            waits=[("pe_dn", dp_guard[dpar])], inc=f"dpld{dpar}",
            dma=True)

        for ic in range(NI):
            sl = gslots[ic // 4]
            off = (ic % 4) * 128
            for th in range(2):
                gp = th

                def gmm(e, ic=ic, th=th, sl=sl, off=off, gp=gp, par=par):
                    r = None
                    for ci in range(NCH):
                        r = e.matmul(
                            pG[gp][:],
                            ring[sl][:, ci, off:off + 128],
                            y1fm[par][:, ci, th * 512:(th + 1) * 512],
                            start=(ci == 0), stop=(ci == NCH - 1))
                    return r
                P.op("tensor", gmm,
                     waits=[(f"ring{sl}", None), f"y1fm{par}", "gc",
                            (f"silu{gp}", None)],
                     inc="pe_gu")
                P.op("scalar",
                     lambda e, ic=ic, th=th, gp=gp: e.activation(
                         sg_sb[:, ic, th * 512:(th + 1) * 512],
                         pG[gp][:], AF.Silu),
                     waits=[("pe_gu", None), ("uact0", None),
                            ("uact1", None)],
                     inc=f"silu{gp}")
            slot_state[sl] = P.count("pe_gu")
        uslots = [load_ring(usrc[:, j * 512:j * 512 + w]
                            .rearrange("(c p) i -> p c i", p=128))
                  for j, w in enumerate(widths)]
        for ic in range(NI):
            sl = uslots[ic // 4]
            off = (ic % 4) * 128
            for th in range(2):
                up = th

                def umm(e, ic=ic, th=th, sl=sl, off=off, up=up, par=par):
                    r = None
                    for ci in range(NCH):
                        r = e.matmul(
                            pU[up][:],
                            ring[sl][:, ci, off:off + 128],
                            y1fm[par][:, ci, th * 512:(th + 1) * 512],
                            start=(ci == 0), stop=(ci == NCH - 1))
                    return r
                P.op("tensor", umm,
                     waits=[(f"ring{sl}", None), (uact_names[up], None),
                            ("pe_glog", None), ("glog", None)],
                     inc="pe_gu")
                P.op("vector",
                     lambda e, ic=ic, th=th, up=up: e.tensor_tensor(
                         act_sb[:, ic, th * 512:(th + 1) * 512],
                         pU[up][:],
                         sg_sb[:, ic, th * 512:(th + 1) * 512],
                         OP.mult),
                     waits=[("pe_gu", None), (f"silu{up}", None),
                            ("pe_dn", None)], inc=uact_names[up])
            slot_state[sl] = P.count("pe_gu")

        if "actd" in d and tsb == 0 and ee == 0:
            P.op("sync", lambda e: e.dma_start(d["actd"][:], act_sb[:]),
                 waits=["uact0", "uact1"], inc="dvst", dma=True)
            P.op("sync", lambda e: e.dma_start(d["sgd"][:], sg_sb[:]),
                 waits=["silu0", "silu1"], inc="dvst", dma=True)
        for i8 in range(NT):
            yp = i8 % 2

            def dmm(e, i8=i8, yp=yp, dpar=dpar):
                r = None
                for ic in range(NI):
                    for n in range(2):
                        r = e.matmul(
                            pY[yp][:, n * 512:(n + 1) * 512],
                            act_sb[:, ic, i8 * 128:(i8 + 1) * 128],
                            dpb[dpar][:, ic, n * 512:(n + 1) * 512],
                            start=(ic == 0), stop=(ic == NI - 1))
                return r
            P.op("tensor", dmm,
                 waits=[("uact0", None), ("uact1", None),
                        (f"dpld{dpar}", None), (f"yadd{yp}", None)],
                 inc="pe_dn")
            if routed and ee == 0:
                P.op("vector", lambda e, i8=i8, yp=yp, par=par:
                     e.tensor_scalar(
                         yacc[:, i8, :], pY[yp][:], cw[par][:, i8, 0:1],
                         None, OP.mult),
                     waits=[("pe_dn", None), ("gdve", None),
                            ("outw", None)], inc=f"yadd{yp}")
            elif routed:
                P.op("vector",
                     lambda e, i8=i8, yp=yp, ee=ee, par=par:
                     e.scalar_tensor_tensor(
                         yacc[:, i8, :], pY[yp][:],
                         cw[par][:, i8, ee:ee + 1], yacc[:, i8, :],
                         OP.mult, OP.add),
                     waits=[("pe_dn", None)], inc=f"yadd{yp}")
            else:
                P.op("vector", lambda e, i8=i8, yp=yp: e.tensor_tensor(
                    yacc[:, i8, :], pY[yp][:], yacc[:, i8, :], OP.add),
                    waits=[("pe_dn", None)], inc=f"yadd{yp}")
        dp_guard[dpar] = P.count("pe_dn")

    def finalize_emit(tsb):
        t0 = tsb * TSB
        if "yad" in d:
            P.op("sync", lambda e, tsb=tsb: e.dma_start(
                d["yad"][tsb], yacc[:]),
                waits=["yadd0", "yadd1"], inc="dvst", dma=True)
        for i8 in range(NT):
            pr = i8 % 2
            tg = t0 + i8 * 128
            P.op("sync", lambda e, tg=tg, pr=pr: e.dma_start(
                y1tm[pr][:], d["y1f"][tg:tg + 128, :]),
                waits=[(f"fln{pr}", None)], inc=f"y1tm{pr}", dma=True)
            P.op("vector", lambda e, i8=i8, pr=pr: e.tensor_tensor(
                yacc[:, i8, :], y1tm[pr][:], yacc[:, i8, :], OP.add),
                waits=[(f"y1tm{pr}", None), ("yadd0", None),
                       ("yadd1", None)], inc=f"fln{pr}")
            P.op("vector", lambda e, i8=i8: e.reduce_sum(
                stA[:, i8:i8 + 1], yacc[:, i8, :], axis=AX.X),
                waits=[(f"fln{pr}", None)], inc="flnr")
            P.op("scalar", lambda e, i8=i8: e.activation(
                sqd[:], yacc[:, i8, :], AF.Square,
                accum_out=stB[:, i8:i8 + 1]),
                waits=[(f"fln{pr}", None), ("flnsq", None)], inc="flnsq")
        def stats(e):
            ch = g["chV"]
            cstep(e, ch, e.tensor_scalar(stA[:], stA[:], 1.0 / D, None,
                                         OP.mult))
            cstep(e, ch, e.tensor_scalar(stB[:], stB[:], 1.0 / D, None,
                                         OP.mult))
            cstep(e, ch, e.tensor_mul(stC[:], stA[:], stA[:]))
            cstep(e, ch, e.tensor_sub(stB[:], stB[:], stC[:]))
            return e.tensor_scalar(stB[:], stB[:], EPS, None, OP.add)
        P.op("vector", stats, waits=[("flnr", None), ("flnsq", None)],
             inc="flns")

        def rstd8(e):
            cstep(e, g["chS"], e.activation(stC[:], stB[:], AF.Ln))
            return e.activation(stB[:], stC[:], AF.Exp, scale=-0.5)
        P.op("scalar", rstd8, waits=[("flns", None)], inc="flna")
        for i8 in range(NT):
            pr = i8 % 2
            tg = t0 + i8 * 128
            P.op("vector", lambda e, i8=i8, pr=pr: e.tensor_scalar(
                outb[pr][:], yacc[:, i8, :], stA[:, i8:i8 + 1],
                stB[:, i8:i8 + 1], OP.subtract, OP.mult),
                waits=[("flna", None), (f"outst{pr}", None)],
                inc="outw")
            P.op("sync", lambda e, tg=tg, pr=pr: e.dma_start(
                d["out"][tg:tg + 128, :], outb[pr][:]),
                waits=[("outw", None)], inc=f"outst{pr}", dma=True)

    for tsb in range(NTSB):
        if tsb == 0:
            gate_emit(0)
            for i8 in range(NT):
                gchain_emit(i8, 0)
            gate_state[0] = None
        for ee in range(DEV_NEXPERTS):
            expert_emit(tsb, ee)
            if ee == 7 and tsb + 1 < NTSB and DEV_NEXPERTS > 9:
                gate_emit(tsb + 1)
            if ee >= 9 and gate_state[0] is not None and ee <= 16:
                gtsb, gpar = gate_state[0]
                gchain_emit(ee - 9, gpar)
                if ee == 16:
                    gate_state[0] = None
        if DEV_NEXPERTS <= 9 and tsb + 1 < NTSB:
            gate_emit(tsb + 1)
            for i8 in range(NT):
                gchain_emit(i8, (tsb + 1) % 2)
            gate_state[0] = None
        finalize_emit(tsb)


_prog_cache = {}


def _get_prog(dev_phase=3):
    key = dev_phase
    if key not in _prog_cache:
        _prog_cache[key] = build_program(dev_phase=dev_phase)
    return _prog_cache[key]


def _stage_weights(inputs):
    asc = np.ascontiguousarray
    f = np.float32

    def a(x):
        return asc(np.asarray(x, dtype=f))
    w = {
        "wq": a(inputs["L0_attn_W_q"]),
        "wk": a(inputs["L0_attn_W_k"]),
        "wv": a(inputs["L0_attn_W_v"]),
        "wo": a(inputs["L0_attn_W_o"]),
        "wdown": a(inputs["L0_attn_W_down_kv"]),
        "wupk": a(inputs["L0_attn_W_up_k"]),
        "wupv": a(inputs["L0_attn_W_up_v"]),
        "gwT": a(np.asarray(inputs["L0_ffn_gate_w"], dtype=f).T),
        "gbias": a(np.asarray(inputs["L0_ffn_gate_bias"],
                              dtype=f).reshape(1, E)),
        "gpT": a(np.asarray(inputs["L0_ffn_gate_proj"],
                            dtype=f).transpose(0, 2, 1)),
        "upT": a(np.asarray(inputs["L0_ffn_up_proj"],
                            dtype=f).transpose(0, 2, 1)),
        "dpT": a(np.asarray(inputs["L0_ffn_down_proj"],
                            dtype=f).transpose(0, 2, 1)),
        "sgT": a(np.asarray(inputs["L0_ffn_shared_gate_w"], dtype=f).T),
        "suT": a(np.asarray(inputs["L0_ffn_shared_up_w"], dtype=f).T),
        "sdT": a(np.asarray(inputs["L0_ffn_shared_down_w"], dtype=f).T),
    }
    t = np.arange(S, dtype=f)
    w["decay"] = np.exp(-LAM * np.abs(t[:, None] - t[None, :])).astype(
        ml_dtypes.bfloat16)
    w["tinyc"] = (-(np.arange(E, dtype=f) + 1) * 1e-30).reshape(1, E)
    return w


def kernel(**inputs):
    nc = _get_prog(3)
    w = _stage_weights(inputs)
    x = np.asarray(inputs["L0_attn_x"], dtype=np.float32)
    in_maps = []
    for c in range(NCORES):
        m = dict(w)
        m["x"] = np.ascontiguousarray(
            x[c * B_PER_CORE:(c + 1) * B_PER_CORE].reshape(T, D))
        in_maps.append(m)
    res = run_bass_kernel_spmd(nc, in_maps, core_ids=list(range(NCORES)))
    out = np.concatenate(
        [r["out"].reshape(B_PER_CORE, S, D) for r in res.results], axis=0)
    return out
